# version 01
# baseline (speedup 1.0000x reference)
import numpy as np
import concourse.bass as bass
import concourse.mybir as mybir
from concourse.bass_utils import run_bass_kernel_spmd

F32 = mybir.dt.float32
BF16 = mybir.dt.bfloat16
AF = mybir.ActivationFunctionType
ALU = mybir.AluOpType
AX = mybir.AxisListType

DM = 1024
E2 = 4096
E = 2048
NST = 16
NE = 15
FF = 1024
FS = 2048
BLOC = 4
L = 2048
TS = 1024
NSB = BLOC * L // TS
NTC = TS // 128
NEC = E // 128
NDMC = DM // 128
ECG = 2
NG = NEC // ECG
EPS = 1e-5

ENGINES = ("sync", "vector", "scalar", "tensor", "gpsimd")


class Prog:
    def __init__(self):
        self.ops = {e: [] for e in ENGINES}
        self.cnt = {}

    def tok(self, sem):
        return (sem, self.cnt.get(sem, 0))

    def emit(self, eng, fn=None, waits=(), inc=None):
        tok = None
        incv = None
        if inc is not None:
            sem, d = inc
            self.cnt[sem] = self.cnt.get(sem, 0) + d
            tok = (sem, self.cnt[sem])
            incv = (sem, d)
        ws = tuple(w for w in waits if w is not None and w[1] > 0)
        self.ops[eng].append((ws, fn, incv))
        return tok


PHASES = ("p0", "A", "B", "Ca", "C", "D", "E", "F", "G", "H")


def build_core_program(parts=("attn", "moe", "shared", "resid"), debug=False,
                       nsb=NSB, stop="H", ca_mode=3):
    stop_i = PHASES.index(stop)
    nc = bass.Bass("TRN2", target_bir_lowering=False)
    pr = Prog()

    def din(name, shape):
        return nc.dram_tensor(name, list(shape), F32, kind="ExternalInput").ap()

    d_x = din("x", (BLOC, L, DM))
    d_inproj = din("w_inproj", (E2, DM))
    d_convw = din("w_conv", (E, 1, 4))
    d_convb = din("b_conv", (E,))
    d_dtw = din("w_dt", (E, E))
    d_dtb = din("b_dt", (E,))
    d_wB = din("w_B", (NST, E))
    d_wC = din("w_C", (NST, E))
    d_Alog = din("A_log", (E, NST))
    d_D = din("D_par", (E,))
    d_outp = din("w_outp", (DM, E))
    d_gate = din("w_gate", (NE, DM))
    d_fc1 = din("w_fc1", (NE, 2 * FF, DM))
    d_fc1b = din("b_fc1", (NE, 2 * FF))
    d_gmult = din("g_mult", (NE, FF))
    d_fc2 = din("w_fc2", (NE, DM, FF))
    d_fc2b = din("b_fc2", (NE, DM))
    d_sfc1 = din("w_sfc1", (2 * FS, DM))
    d_sfc1b = din("b_sfc1", (2 * FS,))
    d_smult = din("s_mult", (FS,))
    d_sfc2 = din("w_sfc2", (DM, FS))
    d_sfc2b = din("b_sfc2", (DM,))
    d_ident = nc.dram_tensor("ident_bf", [128, 128], BF16, kind="ExternalInput").ap()
    d_identf = nc.dram_tensor("ident_f", [128, 128], F32, kind="ExternalInput").ap()
    d_ones = nc.dram_tensor("ones_bf", [1, 128], BF16, kind="ExternalInput").ap()

    d_out = nc.dram_tensor("out", [BLOC, L, DM], F32, kind="ExternalOutput").ap()

    def dint(name, shape, dt):
        return nc.dram_tensor(name, list(shape), dt, kind="Internal").ap()

    sc_n = {}
    for nm, shp in [("inproj", (E2, DM)), ("dtw", (E, E)), ("outp", (DM, E)),
                    ("sfc1", (2 * FS, DM)), ("sfc2", (DM, FS)),
                    ("wB", (NST, E)), ("wC", (NST, E)), ("gate", (NE, DM))]:
        sc_n[nm] = dint("scn_" + nm, shp, BF16)
    sc_n["fc1"] = dint("scn_fc1", (NE, 2 * FF, DM), BF16)
    sc_n["fc2"] = dint("scn_fc2", (NE, DM, FF), BF16)
    sc_inprojT = dint("sc_inprojT", (DM, E2), BF16)
    sc_dtwT = dint("sc_dtwT", (E, E), BF16)
    sc_outpT = dint("sc_outpT", (E, DM), BF16)
    sc_fc1T = dint("sc_fc1T", (NE, DM, 2 * FF), BF16)
    sc_fc2T = dint("sc_fc2T", (NE, FF, DM), BF16)
    sc_sfc1T = dint("sc_sfc1T", (DM, 2 * FS), BF16)
    sc_sfc2T = dint("sc_sfc2T", (FS, DM), BF16)
    sp_dt = dint("sp_dt", (NEC, 128, TS), F32)
    sp_xs = dint("sp_xs", (NEC, 128, TS), F32)
    sp_z = dint("sp_z", (NEC, 128, TS), F32)
    sp_ym = dint("sp_ym", (NEC, 128, TS), BF16)
    sc_bc = dint("sc_bc", (2 * NST, TS), F32)

    kind_dbg = "ExternalOutput" if debug else "Internal"
    d_tap_comb = nc.dram_tensor("tap_comb", [128, NTC * NE], F32, kind=kind_dbg).ap()
    d_tap_bc = nc.dram_tensor("tap_bc", [2 * NST, TS], F32, kind=kind_dbg).ap()
    d_tap_att = nc.dram_tensor("tap_att", [128, NTC * DM], F32, kind=kind_dbg).ap()
    d_tap_xnT = nc.dram_tensor("tap_xnT", [128, NDMC * TS], BF16, kind=kind_dbg).ap()
    d_tap_lnst = nc.dram_tensor("tap_lnst", [128, 32], F32, kind=kind_dbg).ap()

    import contextlib
    stack = contextlib.ExitStack()

    def sbuf(name, shape, dt=F32):
        return stack.enter_context(nc.sbuf_tensor(name, list(shape), dt))

    w8 = sbuf("w8", (128, 24576), BF16)
    wB4 = sbuf("wB4", (128, 16384), BF16)
    xnT = sbuf("xnT", (128, NDMC * TS), BF16)
    yacc = sbuf("yacc", (128, NTC * DM), F32)
    trio = sbuf("trio", (128, 12288), BF16)
    ymst = sbuf("ymst", (128, 2 * TS), BF16)
    stg = sbuf("stg", (128, 2 * TS), F32)
    rlbuf = sbuf("rlbuf", (128, 2 * TS), F32)
    bcrep = sbuf("bcrep", (128, 2 * TS), F32)
    tmpa = sbuf("tmpa", (128, TS), F32)
    tmpb = sbuf("tmpb", (128, TS), F32)
    tmph = sbuf("tmph", (128, TS), F32)
    combT = sbuf("combT", (NST, TS), BF16)
    idb = sbuf("idb", (128, 128), BF16)
    gateT = sbuf("gateT", (128, NDMC * NE), BF16)
    bcT = sbuf("bcT", (128, NEC * 2 * NST), BF16)
    fc2bias = sbuf("fc2bias", (NE, DM), BF16)
    sfc2bias = sbuf("sfc2bias", (1, DM), BF16)
    onesb = sbuf("onesb", (1, 128), BF16)
    combb = sbuf("combb", (128, NE), BF16)
    xpadv = rlbuf[:, 0:TS + 3]
    sqtv = rlbuf[:, TS:2 * TS]
    tails = sbuf("tails", (128, NEC * 3), F32)
    hstate = sbuf("hstate", (128, NEC * NST), F32)
    idf = sbuf("idf", (128, 128), F32)
    A_sb = sbuf("A_sb", (128, NEC * NST), F32)
    convw = sbuf("convw", (128, NEC * 4), F32)
    convb = sbuf("convb", (128, NEC), F32)
    dtb = sbuf("dtb", (128, NEC), F32)
    D_sb = sbuf("D_sb", (128, NEC), F32)
    gmult = sbuf("gmult", (128, NE * 8), F32)
    fc1bs = sbuf("fc1bs", (128, NE * 16), F32)
    sfc1bs = sbuf("sfc1bs", (128, 32), F32)
    smult = sbuf("smult", (128, NST), F32)
    comb = sbuf("comb", (128, NTC * NE), F32)
    lnst = sbuf("lnst", (128, 32), F32)

    trioF = trio[:].bitcast(F32)

    def dtg(i):
        return trioF[:, i * TS:(i + 1) * TS]

    def ug(i):
        return trioF[:, (ECG + i) * TS:(ECG + i + 1) * TS]

    def ysg(i):
        return trioF[:, (2 * ECG + i) * TS:(2 * ECG + i + 1) * TS]

    def apbv(j):
        return trio[:, j * TS:(j + 1) * TS]

    psA = stack.enter_context(nc.psum_tensor("psA", [128, 1024], F32))
    psB = stack.enter_context(nc.psum_tensor("psB", [128, 1024], F32))
    psC = stack.enter_context(nc.psum_tensor("psC", [128, 1024], F32))
    psT = stack.enter_context(nc.psum_tensor("psT", [128, 1024], BF16))
    psG = stack.enter_context(nc.psum_tensor("psG", [128, 512], F32))
    PS = {"psA": psA, "psB": psB, "psC": psC}

    sems = {}
    for s in ["s_in", "s_w", "s_sp", "s_rl", "s_bc", "s_out", "s_pe", "s_dv",
              "s_ac", "s_gp", "s_ym", "s_x", "s_wt", "s_wa", "s_wb", "s_w2", "s_tap"]:
        sems[s] = stack.enter_context(nc.semaphore(s))

    E_ = pr.emit
    ps_free = {"psA": [], "psB": [], "psC": [], "psT": [], "psG": []}

    cast_list = [("inproj", sc_n["inproj"], d_inproj), ("dtw", sc_n["dtw"], d_dtw),
                 ("outp", sc_n["outp"], d_outp), ("sfc1", sc_n["sfc1"], d_sfc1),
                 ("sfc2", sc_n["sfc2"], d_sfc2), ("wB", sc_n["wB"], d_wB),
                 ("wC", sc_n["wC"], d_wC), ("gate", sc_n["gate"], d_gate)]
    cast_tok = {}
    for nm, dst, src in cast_list:
        cast_tok[nm] = E_("gpsimd", lambda g, dst=dst, src=src: g.dma_start(dst, src), inc=("s_gp", 16))
    for e in range(NE):
        cast_tok[("fc1", e)] = E_("gpsimd", lambda g, e=e: g.dma_start(sc_n["fc1"][e], d_fc1[e]), inc=("s_gp", 16))
        cast_tok[("fc2", e)] = E_("gpsimd", lambda g, e=e: g.dma_start(sc_n["fc2"][e], d_fc2[e]), inc=("s_gp", 16))
    E_("gpsimd", lambda g: g.dma_start(fc2bias[:], d_fc2b), inc=("s_gp", 16))
    E_("gpsimd", lambda g: g.dma_start(sfc2bias[:], d_sfc2b.rearrange("(o d) -> o d", o=1)), inc=("s_gp", 16))
    t_gp_all = pr.tok("s_gp")

    E_("sync", lambda s: s.dma_start(idb[:], d_ident), inc=("s_in", 16))
    E_("sync", lambda s: s.dma_start(idf[:], d_identf), inc=("s_in", 16))
    E_("sync", lambda s: s.dma_start(onesb[:], d_ones), inc=("s_in", 16))
    for ec in range(NEC):
        E_("sync", lambda s, ec=ec: s.dma_start(A_sb[:, ec * NST:(ec + 1) * NST],
                                                d_Alog[ec * 128:(ec + 1) * 128, :]), inc=("s_in", 16))
        E_("sync", lambda s, ec=ec: s.dma_start(convw[:, ec * 4:(ec + 1) * 4],
                                                d_convw[ec * 128:(ec + 1) * 128, 0, :]), inc=("s_in", 16))
    t_in_small = pr.tok("s_in")

    vec_list = [
        (d_convb.rearrange("(a p) -> a p", p=128), convb, 0, NEC),
        (d_dtb.rearrange("(a p) -> a p", p=128), dtb, 0, NEC),
        (d_D.rearrange("(a p) -> a p", p=128), D_sb, 0, NEC),
        (d_sfc1b.rearrange("(a p) -> a p", p=128), sfc1bs, 0, 32),
        (d_smult.rearrange("(a p) -> a p", p=128), smult, 0, NST),
        (d_gmult.rearrange("e (a p) -> (e a) p", p=128), gmult, 0, NE * 8),
        (d_fc1b.rearrange("e (a p) -> (e a) p", p=128), fc1bs, 0, NE * 16),
    ]
    vs_view = rlbuf[:, 0:128]
    for src, dst, coff, k in vec_list:
        done = 0
        while done < k:
            g = min(120, k - done)
            t_ld = E_("sync", lambda s, src=src, done=done, g=g:
                      s.dma_start(vs_view[0:g, 0:128], src[done:done + g, :]),
                      waits=[pr.tok("s_pe")], inc=("s_in", 16))
            t_pe = E_("tensor", lambda t, g=g: t.transpose(psA[:, 0:g], vs_view[0:g, 0:128], idf[0:g, 0:g]),
                      waits=[t_ld, pr.tok("s_ac")], inc=("s_pe", 1))
            E_("scalar", lambda sc, dst=dst, coff=coff, done=done, g=g:
               sc.copy(dst[:, coff + done:coff + done + g], psA[:, 0:g]),
               waits=[t_pe], inc=("s_ac", 1))
            done += g

    def small_T(src_dram, nrow, ncolchunks, dst, dstw, coff, wait_tok):
        for c in range(ncolchunks):
            t_ld = E_("sync", lambda s, src_dram=src_dram, c=c, nrow=nrow:
                      s.dma_start(ymst[0:nrow, 0:128], src_dram[:, c * 128:(c + 1) * 128]),
                      waits=[wait_tok, pr.tok("s_pe")], inc=("s_in", 16))
            t_pe = E_("tensor", lambda t, nrow=nrow: t.transpose(psT[:, 0:nrow], ymst[0:nrow, 0:128],
                                                                 idb[0:nrow, 0:nrow]),
                      waits=[t_ld, pr.tok("s_ac")], inc=("s_pe", 1))
            E_("scalar", lambda sc, dst=dst, c=c, dstw=dstw, coff=coff, nrow=nrow:
               sc.copy(dst[:, c * dstw + coff:c * dstw + coff + nrow], psT[:, 0:nrow]),
               waits=[t_pe], inc=("s_ac", 1))

    small_T(sc_n["gate"], NE, NDMC, gateT, NE, 0, t_gp_all)
    small_T(sc_n["wB"], NST, NEC, bcT, 2 * NST, 0, t_gp_all)
    small_T(sc_n["wC"], NST, NEC, bcT, 2 * NST, NST, t_gp_all)

    big_T = [
        (sc_n["inproj"], cast_tok["inproj"], sc_inprojT, E2, NDMC),
        (sc_n["dtw"], cast_tok["dtw"], sc_dtwT, E, NEC),
        (sc_n["outp"], cast_tok["outp"], sc_outpT, DM, NEC),
        (sc_n["sfc1"], cast_tok["sfc1"], sc_sfc1T, 2 * FS, NDMC),
        (sc_n["sfc2"], cast_tok["sfc2"], sc_sfc2T, DM, NEC),
    ]
    for e in range(NE):
        big_T.append((sc_n["fc1"][e], cast_tok[("fc1", e)], sc_fc1T[e], 2 * FF, NDMC))
        big_T.append((sc_n["fc2"][e], cast_tok[("fc2", e)], sc_fc2T[e], DM, 8))
    slot_busy = [None, None]
    slot_sem = ["s_wa", "s_wb"]
    sidx = 0
    for src, ctok, dstT, nrows, nchunks in big_T:
        for c in range(nchunks):
            sl = sidx % 2
            sidx += 1
            view = w8[:, sl * 4096: sl * 4096 + nrows]
            t_tr = E_("sync", lambda s, view=view, src=src, c=c, nrows=nrows:
                      s.dma_start_transpose(view, src[0:nrows, c * 128:(c + 1) * 128]),
                      waits=[t_gp_all] + ([slot_busy[sl]] if slot_busy[sl] else []), inc=("s_wt", 16))
            slot_busy[sl] = E_("sync", lambda s, dstT=dstT, c=c, view=view, nrows=nrows:
                               s.dma_start(dstT[c * 128:(c + 1) * 128, :], view),
                               waits=[t_tr], inc=(slot_sem[sl], 16))
    t_preT = [pr.tok("s_wa"), pr.tok("s_wb")]

    t_Ae = E_("scalar", lambda sc: sc.activation(A_sb[:], A_sb[:], AF.Exp),
              waits=[pr.tok("s_in")], inc=("s_ac", 1))
    E_("vector", lambda v: v.tensor_scalar_mul(A_sb[:], A_sb[:], -1.0),
       waits=[t_Ae], inc=("s_dv", 1))

    t_p0_ac = pr.tok("s_ac")
    t_p0_in = pr.tok("s_in")
    E_("vector", lambda v: v.memset(hstate[:], 0.0), inc=("s_dv", 1))
    E_("vector", lambda v: v.memset(tails[:], 0.0), inc=("s_dv", 1))
    t_p0_dv = pr.tok("s_dv")

    def w8v(c0, w):
        return w8[:, c0:c0 + w]

    def wB4v(c0, w):
        return wB4[:, c0:c0 + w]

    xsbv = wB4

    prev_w8_free = None
    prev_wB4_free = None

    for sb_i in range(nsb):
        b = sb_i // 2
        half = sb_i % 2
        tb = half * TS
        first = sb_i == 0

        w_wait = t_preT if first else [prev_w8_free]
        t_w_inproj = None
        for dmc in range(NDMC):
            t_w_inproj = E_("sync", lambda s, dmc=dmc: s.dma_start(w8v(dmc * 2048, 2048),
                                                                   sc_inprojT[dmc * 128:(dmc + 1) * 128, 0:2048]),
                            waits=w_wait, inc=("s_w", 16))

        if stop_i < PHASES.index("A"):
            continue
        waitsA = [t_p0_dv, t_p0_in, t_gp_all, t_p0_ac] if first else []
        for tc in range(NTC):
            xv = stg[:, 0:DM]
            t_x = E_("sync", lambda s, b=b, tb=tb, tc=tc, xv=xv:
                     s.dma_start(xv, d_x[b, tb + tc * 128: tb + (tc + 1) * 128, :]),
                     waits=waitsA + [pr.tok("s_dv"), pr.tok("s_ac"), pr.tok("s_tap")], inc=("s_x", 16))
            c0 = (tc % 2) * 8
            t1 = E_("vector", lambda v, xv=xv, c0=c0: v.reduce_sum(lnst[:, c0:c0 + 1], xv, axis=AX.X),
                    waits=[t_x], inc=("s_dv", 1))
            t2 = E_("vector", lambda v, c0=c0: v.tensor_scalar_mul(lnst[:, c0 + 1:c0 + 2], lnst[:, c0:c0 + 1], 1.0 / DM),
                    waits=[t1], inc=("s_dv", 1))
            t3 = E_("vector", lambda v, xv=xv, c0=c0: v.tensor_single_scalar(sqtv, xv, lnst[:, c0 + 1:c0 + 2],
                                                                             op=ALU.subtract),
                    waits=[t2, pr.tok("s_ac"), pr.tok("s_rl")], inc=("s_dv", 1))
            t4 = E_("scalar", lambda sc: sc.activation(sqtv, sqtv, AF.Square), waits=[t3], inc=("s_ac", 1))
            t5 = E_("vector", lambda v, c0=c0: v.reduce_sum(lnst[:, c0 + 2:c0 + 3], sqtv, axis=AX.X),
                    waits=[t4], inc=("s_dv", 1))
            t5b = E_("vector", lambda v, c0=c0: v.tensor_scalar(lnst[:, c0 + 2:c0 + 3], lnst[:, c0 + 2:c0 + 3],
                                                                1.0 / DM, EPS, op0=ALU.mult, op1=ALU.add),
                     waits=[t5], inc=("s_dv", 1))
            t6 = E_("scalar", lambda sc, c0=c0: sc.activation(lnst[:, c0 + 3:c0 + 4], lnst[:, c0 + 2:c0 + 3],
                                                              AF.Ln),
                    waits=[t5b], inc=("s_ac", 1))
            t7 = E_("vector", lambda v, c0=c0: v.tensor_scalar_mul(lnst[:, c0 + 4:c0 + 5], lnst[:, c0 + 3:c0 + 4], -0.5),
                    waits=[t6], inc=("s_dv", 1))
            t8 = E_("scalar", lambda sc, c0=c0: sc.activation(lnst[:, c0 + 5:c0 + 6], lnst[:, c0 + 4:c0 + 5], AF.Exp),
                    waits=[t7], inc=("s_ac", 1))
            t9 = E_("vector", lambda v, xv=xv, c0=c0:
                    v.tensor_scalar(sqtv, xv, lnst[:, c0 + 1:c0 + 2], lnst[:, c0 + 5:c0 + 6],
                                    op0=ALU.subtract, op1=ALU.mult),
                    waits=[t8], inc=("s_dv", 1))
            t10 = E_("vector", lambda v: v.tensor_copy(ymst[:, 0:DM], sqtv),
                     waits=[pr.tok("s_pe")], inc=("s_dv", 1))
            t_tr = E_("tensor", lambda t: [t.transpose(psT[:, dmc * 128:(dmc + 1) * 128],
                                                       ymst[:, dmc * 128:(dmc + 1) * 128], idb[:])
                                           for dmc in range(NDMC)][-1],
                      waits=[t10] + ps_free["psT"], inc=("s_pe", 1))
            t_cp = E_("scalar", lambda sc, tc=tc:
                      sc.copy(xnT.rearrange("p (a t) -> p a t", a=NDMC)[:, :, tc * 128:(tc + 1) * 128],
                              psT.rearrange("p (a t) -> p a t", a=NDMC)[:, :, 0:128]),
                      waits=[t_tr], inc=("s_ac", 1))
            ps_free["psT"] = [t_cp]

            t_g = E_("tensor", lambda t, tc=tc: [t.matmul(psG[:, 0:NE],
                                                          xnT[:, dmc * TS + tc * 128: dmc * TS + (tc + 1) * 128],
                                                          gateT[:, dmc * NE:(dmc + 1) * NE],
                                                          start=(dmc == 0), stop=(dmc == NDMC - 1))
                                                 for dmc in range(NDMC)][-1],
                     waits=[t_cp] + ps_free["psG"], inc=("s_pe", 1))
            cc = tc * NE
            t_sm1 = E_("vector", lambda v: v.reduce_max(lnst[:, 16:17], psG[:, 0:NE], axis=AX.X),
                       waits=[t_g, pr.tok("s_ac")], inc=("s_dv", 1))
            t_negm = E_("vector", lambda v: v.tensor_scalar_mul(lnst[:, 17:18], lnst[:, 16:17], -1.0),
                        waits=[t_sm1], inc=("s_dv", 1))
            t_sm2 = E_("scalar", lambda sc, cc=cc: sc.activation(comb[:, cc:cc + NE], psG[:, 0:NE], AF.Exp,
                                                                 bias=lnst[:, 17:18]),
                       waits=[t_negm], inc=("s_ac", 1))
            t_sm3 = E_("vector", lambda v, cc=cc: v.reduce_sum(lnst[:, 18:19], comb[:, cc:cc + NE], axis=AX.X),
                       waits=[t_sm2], inc=("s_dv", 1))
            t_rc = E_("vector", lambda v: v.reciprocal(lnst[:, 19:20], lnst[:, 18:19]),
                      waits=[t_sm3], inc=("s_dv", 1))
            t_scl = E_("vector", lambda v, cc=cc: v.tensor_single_scalar(comb[:, cc:cc + NE], comb[:, cc:cc + NE],
                                                                         lnst[:, 19:20], op=ALU.mult),
                       waits=[t_rc], inc=("s_dv", 1))
            ps_free["psG"] = [t_sm3, t_sm2]
            t_k0 = E_("vector", lambda v, cc=cc: v.tensor_copy(sqtv[:, 0:NE], comb[:, cc:cc + NE]),
                      waits=[t_scl], inc=("s_dv", 1))
            t_k1 = E_("vector", lambda v: v.memset(sqtv[:, 16:16 + NE], 0.0), inc=("s_dv", 1))
            t_k2 = E_("vector", lambda v: v.memset(lnst[:, 20:21], 0.0), inc=("s_dv", 1))
            t_prev = t_k2
            for r in range(3):
                t_rm = E_("vector", lambda v: v.reduce_max(lnst[:, 21:22], sqtv[:, 0:NE], axis=AX.X),
                          waits=[t_prev], inc=("s_dv", 1))
                t_eq = E_("vector", lambda v: v.tensor_single_scalar(sqtv[:, 32:32 + NE], sqtv[:, 0:NE],
                                                                     lnst[:, 21:22], op=ALU.is_equal),
                          waits=[t_rm], inc=("s_dv", 1))
                E_("vector", lambda v: v.scalar_tensor_tensor(sqtv[:, 16:16 + NE], sqtv[:, 32:32 + NE],
                                                              lnst[:, 21:22], sqtv[:, 16:16 + NE],
                                                              op0=ALU.mult, op1=ALU.add),
                   waits=[t_eq], inc=("s_dv", 1))
                E_("vector", lambda v: v.tensor_add(lnst[:, 20:21], lnst[:, 20:21], lnst[:, 21:22]),
                   waits=[t_rm], inc=("s_dv", 1))
                t_prev = E_("vector", lambda v: v.scalar_tensor_tensor(sqtv[:, 0:NE], sqtv[:, 32:32 + NE], -1e30,
                                                                       sqtv[:, 0:NE], op0=ALU.mult, op1=ALU.add),
                            waits=[t_eq], inc=("s_dv", 1))
            t_dn = E_("vector", lambda v: v.tensor_scalar_add(lnst[:, 20:21], lnst[:, 20:21], 1e-20),
                      waits=[t_prev], inc=("s_dv", 1))
            t_rc2 = E_("vector", lambda v: v.reciprocal(lnst[:, 22:23], lnst[:, 20:21]),
                       waits=[t_dn], inc=("s_dv", 1))
            t_fin = E_("vector", lambda v, cc=cc: v.tensor_single_scalar(comb[:, cc:cc + NE], sqtv[:, 16:16 + NE],
                                                                         lnst[:, 22:23], op=ALU.mult),
                       waits=[t_rc2], inc=("s_dv", 1))
            t_cb = E_("vector", lambda v, cc=cc: v.tensor_copy(combb[:], comb[:, cc:cc + NE]),
                      waits=[t_fin, pr.tok("s_pe")], inc=("s_dv", 1))
            t_ctr = E_("tensor", lambda t: t.transpose(psT[0:NE, 0:128], combb[:], idb[:]),
                       waits=[t_cb] + ps_free["psT"], inc=("s_pe", 1))
            t_ctc = E_("scalar", lambda sc, tc=tc: sc.copy(combT[0:NE, tc * 128:(tc + 1) * 128], psT[0:NE, 0:128]),
                       waits=[t_ctr], inc=("s_ac", 1))
            ps_free["psT"] = [t_ctc]
        t_comb_done = pr.tok("s_dv")
        t_xnT_done = pr.tok("s_ac")

        if debug and first:
            E_("sync", lambda s: s.dma_start(d_tap_comb, comb[:]), waits=[t_comb_done], inc=("s_tap", 16))
            E_("sync", lambda s: s.dma_start(d_tap_xnT, xnT[:]), waits=[t_xnT_done], inc=("s_tap", 16))

        if stop_i < PHASES.index("B"):
            continue
        for m in range(2 * NEC):
            if m == NEC:
                t_half_pe = pr.tok("s_pe")
                for dmc in range(NDMC):
                    t_w_inproj = E_("sync", lambda s, dmc=dmc:
                                    s.dma_start(w8v(dmc * 2048, 2048),
                                                sc_inprojT[dmc * 128:(dmc + 1) * 128, 2048:4096]),
                                    waits=[t_half_pe], inc=("s_w", 16))
            psn = "psA" if m % 2 == 0 else "psB"
            ps = PS[psn]
            ml = m % NEC
            t_mm = E_("tensor", lambda t, ml=ml, ps=ps: [t.matmul(ps[:, h * 512:(h + 1) * 512],
                                                                  w8v(dmc * 2048 + ml * 128, 128),
                                                                  xnT[:, dmc * TS + h * 512: dmc * TS + (h + 1) * 512],
                                                                  start=(dmc == 0), stop=(dmc == NDMC - 1))
                                                         for h in range(2) for dmc in range(NDMC)][-1],
                      waits=[t_w_inproj, t_xnT_done] + ps_free[psn], inc=("s_pe", 1))
            slot = m % 2
            sv = stg[:, slot * TS:(slot + 1) * TS]
            if m < NEC:
                ec = m
                if half == 0:
                    t_ti = E_("vector", lambda v: v.memset(xpadv[:, 0:3], 0.0),
                              waits=[pr.tok("s_ac"), pr.tok("s_rl")], inc=("s_dv", 1))
                else:
                    t_ti = E_("vector", lambda v, ec=ec: v.tensor_copy(xpadv[:, 0:3], tails[:, ec * 3:(ec + 1) * 3]),
                              waits=[pr.tok("s_ac"), pr.tok("s_rl")], inc=("s_dv", 1))
                t_ev = E_("scalar", lambda sc, ps=ps: sc.copy(xpadv[:, 3:3 + TS], ps[:]),
                          waits=[t_mm, t_ti, pr.tok("s_dv")], inc=("s_ac", 1))
                ps_free[psn] = [t_ev]
                E_("vector", lambda v, ec=ec: v.tensor_copy(tails[:, ec * 3:(ec + 1) * 3], xpadv[:, TS:TS + 3]),
                   waits=[t_ev], inc=("s_dv", 1))
                t_c = E_("vector", lambda v, ec=ec: v.tensor_scalar_mul(tmpb[:], xpadv[:, 0:TS],
                                                                        convw[:, ec * 4:ec * 4 + 1]),
                         inc=("s_dv", 1))
                for k in range(1, 4):
                    t_c = E_("vector", lambda v, ec=ec, k=k: v.scalar_tensor_tensor(tmpb[:], xpadv[:, k:k + TS],
                                                                                    convw[:, ec * 4 + k:ec * 4 + k + 1],
                                                                                    tmpb[:], op0=ALU.mult, op1=ALU.add),
                             inc=("s_dv", 1))
                t_sl = E_("scalar", lambda sc, sv=sv, ec=ec: sc.activation(sv, tmpb[:], AF.Silu,
                                                                           bias=convb[:, ec:ec + 1]),
                          waits=[t_c, pr.tok("s_sp"), pr.tok("s_x"), pr.tok("s_dv")], inc=("s_ac", 1))
                E_("sync", lambda s, sv=sv, ec=ec: s.dma_start(sp_xs[ec], sv), waits=[t_sl], inc=("s_sp", 16))
                E_("vector", lambda v, sv=sv, ec=ec: v.tensor_copy(xsbv[:, ec * TS:(ec + 1) * TS], sv),
                   waits=[t_sl] + ([prev_wB4_free] if (ec == 0 and prev_wB4_free) else []), inc=("s_dv", 1))
            else:
                ec = m - NEC
                t_sl = E_("scalar", lambda sc, sv=sv, ps=ps: sc.activation(sv, ps[:], AF.Silu),
                          waits=[t_mm, pr.tok("s_sp"), pr.tok("s_x"), pr.tok("s_dv")], inc=("s_ac", 1))
                ps_free[psn] = [t_sl]
                E_("sync", lambda s, sv=sv, ec=ec: s.dma_start(sp_z[ec], sv), waits=[t_sl], inc=("s_sp", 16))
        t_phB_sp = pr.tok("s_sp")
        t_phB_dv = pr.tok("s_dv")

        t_last_inproj_mm = pr.tok("s_pe")
        t_w_dt = None
        for ec in range(NEC):
            t_w_dt = E_("sync", lambda s, ec=ec: s.dma_start(w8v(ec * 1024, 1024),
                                                             sc_dtwT[ec * 128:(ec + 1) * 128, 0:1024]),
                        waits=[t_last_inproj_mm], inc=("s_w", 16))

        if stop_i < PHASES.index("Ca"):
            continue
        for m in range(NEC):
            if m == NEC // 2:
                t_half_pe = pr.tok("s_pe")
                for ec in range(NEC):
                    t_w_dt = E_("sync", lambda s, ec=ec:
                                s.dma_start(w8v(ec * 1024, 1024),
                                            sc_dtwT[ec * 128:(ec + 1) * 128, 1024:2048]),
                                waits=[t_half_pe], inc=("s_w", 16))
            psn = "psA" if m % 2 == 0 else "psB"
            ps = PS[psn]
            ml = m % (NEC // 2)
            t_mm = E_("tensor", lambda t, ml=ml, ps=ps: [t.matmul(ps[:, h * 512:(h + 1) * 512],
                                                                  w8v(ec * 1024 + ml * 128, 128),
                                                                  xsbv[:, ec * TS + h * 512: ec * TS + (h + 1) * 512],
                                                                  start=(ec == 0), stop=(ec == NEC - 1))
                                                         for h in range(2) for ec in range(NEC)][-1],
                      waits=[t_w_dt, t_phB_dv] + ps_free[psn], inc=("s_pe", 1))
            if ca_mode >= 2:
                t_a1 = E_("scalar", lambda sc, ps=ps, m=m: sc.activation(tmpa[:], ps[:], AF.Abs, bias=dtb[:, m:m + 1]),
                          waits=[t_mm, pr.tok("s_dv")], inc=("s_ac", 1))
                E_("scalar", lambda sc: sc.activation(tmpa[:], tmpa[:], AF.Exp, scale=-1.0), inc=("s_ac", 1))
                t_a3 = E_("scalar", lambda sc: sc.activation(tmpa[:], tmpa[:], AF.Ln, bias=1.0), inc=("s_ac", 1))
            else:
                t_a1 = E_("scalar", lambda sc, ps=ps, m=m: sc.copy(tmpa[:], ps[:]),
                          waits=[t_mm, pr.tok("s_dv")], inc=("s_ac", 1))
                t_a3 = t_a1
            slot = m % 2
            sv = stg[:, slot * TS:(slot + 1) * TS]
            if ca_mode >= 1:
                t_mx = E_("vector", lambda v, ps=ps, m=m, sv=sv: v.tensor_scalar(sv, ps[:], dtb[:, m:m + 1], 0.0,
                                                                                 op0=ALU.add, op1=ALU.max),
                          waits=[t_mm, t_a1, pr.tok("s_sp")], inc=("s_dv", 1))
            else:
                t_mx = E_("vector", lambda v, sv=sv: v.tensor_copy(sv, tmpb[:]),
                          waits=[t_mm, pr.tok("s_sp"), pr.tok("s_ac")], inc=("s_dv", 1))
            ps_free[psn] = [t_mx, t_a1]
            t_dt = E_("vector", lambda v, sv=sv: v.tensor_add(sv, sv, tmpa[:]), waits=[t_a3], inc=("s_dv", 1))
            E_("sync", lambda s, sv=sv, m=m: s.dma_start(sp_dt[m], sv), waits=[t_dt], inc=("s_sp", 16))
        if stop_i < PHASES.index("C"):
            continue
        t_bc_mm = E_("tensor", lambda t: [t.matmul(psC[0:32, h * 512:(h + 1) * 512],
                                                   bcT[:, ec * 32:(ec + 1) * 32],
                                                   xsbv[:, ec * TS + h * 512: ec * TS + (h + 1) * 512],
                                                   start=(ec == 0), stop=(ec == NEC - 1))
                                          for h in range(2) for ec in range(NEC)][-1],
                     waits=[t_phB_dv, t_p0_ac] + ps_free["psC"], inc=("s_pe", 1))
        t_bc_ev = E_("scalar", lambda sc: sc.copy(stg[0:32, 0:TS], psC[0:32, :]),
                     waits=[t_bc_mm, pr.tok("s_sp")], inc=("s_ac", 1))
        ps_free["psC"] = [t_bc_ev]
        t_bc_st = E_("sync", lambda s: s.dma_start(sc_bc, stg[0:32, 0:TS]), waits=[t_bc_ev], inc=("s_sp", 16))
        t_phC_sp = pr.tok("s_sp")

        if debug and first:
            E_("sync", lambda s: s.dma_start(d_tap_bc, stg[0:32, 0:TS]), waits=[t_bc_ev], inc=("s_tap", 16))

        t_phC_pe = pr.tok("s_pe")
        t_w_outp = None
        for ec in range(NEC):
            t_w_outp = E_("sync", lambda s, ec=ec: s.dma_start(wB4v(ec * 1024, DM),
                                                               sc_outpT[ec * 128:(ec + 1) * 128, :]),
                          waits=[t_phC_pe], inc=("s_w", 16))

        if stop_i < PHASES.index("D"):
            continue
        if half == 0:
            E_("vector", lambda v: v.memset(hstate[:], 0.0), inc=("s_dv", 1))
        for g in range(NG):
            t_rl_grp = None
            for i in range(ECG):
                ec = g * ECG + i
                E_("sync", lambda s, ec=ec, i=i: s.dma_start(dtg(i), sp_dt[ec]),
                   waits=[t_phC_sp, pr.tok("s_dv"), pr.tok("s_pe")], inc=("s_rl", 16))
                t_rl_grp = E_("sync", lambda s, ec=ec, i=i: s.dma_start(rlbuf[:, i * TS:(i + 1) * TS], sp_xs[ec]),
                              waits=[t_phB_sp, pr.tok("s_dv"), pr.tok("s_ac")], inc=("s_rl", 16))
            rl_toks = []
            for i in range(ECG):
                t_u = E_("vector", lambda v, i=i: v.tensor_mul(ug(i), dtg(i), rlbuf[:, i * TS:(i + 1) * TS]),
                         waits=[t_rl_grp], inc=("s_dv", 1))
                rl_toks.append(t_u)
            for n in range(NST):
                E_("sync", lambda s, n=n: s.dma_start(bcrep[:, 0:TS],
                                                      sc_bc[n:n + 1, :].broadcast_to([128, TS])),
                   waits=[t_bc_st, pr.tok("s_dv")], inc=("s_bc", 16))
                t_bC = E_("sync", lambda s, n=n: s.dma_start(bcrep[:, TS:2 * TS],
                                                             sc_bc[NST + n:NST + n + 1, :].broadcast_to([128, TS])),
                          inc=("s_bc", 16))
                for i in range(ECG):
                    ec = g * ECG + i
                    col = ec * NST + n
                    t_aa = E_("scalar", lambda sc, i=i, col=col: sc.activation(tmpa[:], dtg(i), AF.Exp,
                                                                               scale=A_sb[:, col:col + 1]),
                              waits=[rl_toks[i], pr.tok("s_dv")], inc=("s_ac", 1))
                    E_("vector", lambda v, i=i: v.tensor_mul(tmpb[:], ug(i), bcrep[:, 0:TS]),
                       waits=[t_bC], inc=("s_dv", 1))
                    t_scn = E_("vector", lambda v, col=col: v.tensor_tensor_scan(tmph[:], tmpa[:], tmpb[:],
                                                                                 hstate[:, col:col + 1],
                                                                                 op0=ALU.mult, op1=ALU.add),
                               waits=[t_aa], inc=("s_dv", 1))
                    E_("vector", lambda v, col=col: v.tensor_copy(hstate[:, col:col + 1], tmph[:, TS - 1:TS]),
                       waits=[t_scn], inc=("s_dv", 1))
                    if n == 0:
                        E_("vector", lambda v, i=i: v.tensor_mul(ysg(i), tmph[:], bcrep[:, TS:2 * TS]),
                           waits=[t_bC], inc=("s_dv", 1))
                    else:
                        E_("vector", lambda v: v.tensor_mul(tmpa[:], tmph[:], bcrep[:, TS:2 * TS]),
                           waits=[t_bC], inc=("s_dv", 1))
                        E_("vector", lambda v, i=i: v.tensor_add(ysg(i), ysg(i), tmpa[:]), inc=("s_dv", 1))
            t_rl2 = None
            for i in range(ECG):
                ec = g * ECG + i
                E_("sync", lambda s, ec=ec, i=i: s.dma_start(rlbuf[:, i * TS:(i + 1) * TS], sp_xs[ec]),
                   waits=[pr.tok("s_dv")], inc=("s_rl", 16))
                t_rl2 = E_("sync", lambda s, ec=ec, i=i: s.dma_start(stg[:, i * TS:(i + 1) * TS], sp_z[ec]),
                           waits=[t_phC_sp, pr.tok("s_ac"), pr.tok("s_dv"), pr.tok("s_tap")], inc=("s_rl", 16))
            for i in range(ECG):
                ec = g * ECG + i
                t_y2 = E_("vector", lambda v, ec=ec, i=i:
                          v.scalar_tensor_tensor(ysg(i), rlbuf[:, i * TS:(i + 1) * TS],
                                                 D_sb[:, ec:ec + 1], ysg(i), op0=ALU.mult, op1=ALU.add),
                          waits=[t_rl2], inc=("s_dv", 1))
                slot = ec % 2
                t_ym = E_("vector", lambda v, i=i, slot=slot:
                          v.tensor_mul(ymst[:, slot * TS:(slot + 1) * TS], ysg(i), stg[:, i * TS:(i + 1) * TS]),
                          waits=[t_rl2, pr.tok("s_ym")], inc=("s_dv", 1))
                E_("sync", lambda s, ec=ec, slot=slot: s.dma_start(sp_ym[ec], ymst[:, slot * TS:(slot + 1) * TS]),
                   waits=[t_ym], inc=("s_ym", 16))
        t_ym_all = pr.tok("s_ym")

        if stop_i < PHASES.index("E"):
            continue
        for tc in range(NTC):
            t_yl = None
            for ec in range(NEC):
                t_yl = E_("sync", lambda s, ec=ec, tc=tc: s.dma_start(ymst[:, ec * 128:(ec + 1) * 128],
                                                                     sp_ym[ec][:, tc * 128:(tc + 1) * 128]),
                          waits=[t_ym_all] + ([pr.tok("s_pe"), pr.tok("s_dv")] if ec == 0 else []),
                          inc=("s_rl", 16))
            psn = "psA" if tc % 2 == 0 else "psB"
            ps = PS[psn]
            t_mm = E_("tensor", lambda t, ps=ps, t_yl=t_yl: [t.matmul(ps[:, h * 512:(h + 1) * 512],
                                                                     ymst[:, ec * 128:(ec + 1) * 128],
                                                                     wB4v(ec * 1024 + h * 512, 512),
                                                                     start=(ec == 0), stop=(ec == NEC - 1))
                                                             for h in range(2) for ec in range(NEC)][-1],
                      waits=[t_yl, t_w_outp] + ps_free[psn], inc=("s_pe", 1))
            if "attn" in parts:
                t_dep = E_("vector", lambda v, tc=tc, ps=ps: v.tensor_copy(yacc[:, tc * DM:(tc + 1) * DM], ps[:]),
                           waits=[t_mm, pr.tok("s_out")], inc=("s_dv", 1))
            else:
                t_dep = E_("vector", lambda v, tc=tc: v.memset(yacc[:, tc * DM:(tc + 1) * DM], 0.0),
                           waits=[t_mm, pr.tok("s_out")], inc=("s_dv", 1))
            ps_free[psn] = [t_dep]
        t_phE_pe = pr.tok("s_pe")

        if debug and first:
            tdv = pr.tok("s_dv")
            E_("sync", lambda s: s.dma_start(d_tap_att, yacc[:]), waits=[tdv], inc=("s_tap", 16))

        if stop_i < PHASES.index("F"):
            continue
        if "moe" in parts:
            for e in range(NE):
                wf_wait = [t_phE_pe, pr.tok("s_dv"), pr.tok("s_tap")] if e == 0 else [pr.tok("s_pe")]
                tw1 = None
                for dmc in range(NDMC):
                    tw1 = E_("sync", lambda s, dmc=dmc, e=e: s.dma_start(w8v(dmc * 2048, 2 * FF),
                                                                        sc_fc1T[e][dmc * 128:(dmc + 1) * 128, :]),
                             waits=wf_wait, inc=("s_w", 16))
                tw2 = None
                for fc in range(8):
                    tw2 = E_("sync", lambda s, fc=fc, e=e: s.dma_start(w8v(16384 + fc * 1024, DM),
                                                                      sc_fc2T[e][fc * 128:(fc + 1) * 128, :]),
                             waits=wf_wait, inc=("s_w", 16))
                t_sc2 = None
                for fc in range(8):
                    t_sc2 = E_("vector", lambda v, fc=fc, e=e: v.tensor_scalar_mul(w8v(16384 + fc * 1024, DM),
                                                                                   w8v(16384 + fc * 1024, DM),
                                                                                   gmult[:, e * 8 + fc:e * 8 + fc + 1]),
                               waits=[tw2], inc=("s_dv", 1))
                for j in range(8):
                    psgn, psxn = ("psA", "psB") if j % 2 == 0 else ("psB", "psA")
                    psg_, psx_ = PS[psgn], PS[psxn]
                    t_mmg = E_("tensor", lambda t, j=j, psg_=psg_: [t.matmul(psg_[:, h * 512:(h + 1) * 512],
                                                                             w8v(dmc * 2048 + (8 + j) * 128, 128),
                                                                             xnT[:, dmc * TS + h * 512: dmc * TS + (h + 1) * 512],
                                                                             start=(dmc == 0), stop=(dmc == NDMC - 1))
                                                                    for h in range(2) for dmc in range(NDMC)][-1],
                               waits=[tw2] + ps_free[psgn], inc=("s_pe", 1))
                    t_gl = E_("scalar", lambda sc, psg_=psg_, e=e, j=j:
                              sc.activation(stg[:, 0:TS], psg_[:], AF.Gelu,
                                            bias=fc1bs[:, e * 16 + 8 + j:e * 16 + 9 + j]),
                              waits=[t_mmg, pr.tok("s_dv"), pr.tok("s_rl")], inc=("s_ac", 1))
                    ps_free[psgn] = [t_gl]
                    t_mmx = E_("tensor", lambda t, j=j, psx_=psx_: [t.matmul(psx_[:, h * 512:(h + 1) * 512],
                                                                             w8v(dmc * 2048 + j * 128, 128),
                                                                             xnT[:, dmc * TS + h * 512: dmc * TS + (h + 1) * 512],
                                                                             start=(dmc == 0), stop=(dmc == NDMC - 1))
                                                                    for h in range(2) for dmc in range(NDMC)][-1],
                               waits=[tw2] + ps_free[psxn], inc=("s_pe", 1))
                    t_ap = E_("vector", lambda v, psx_=psx_, e=e, j=j:
                              v.scalar_tensor_tensor(apbv(j), psx_[:],
                                                     fc1bs[:, e * 16 + j:e * 16 + j + 1], stg[:, 0:TS],
                                                     op0=ALU.add, op1=ALU.mult),
                              waits=[t_mmx, t_gl], inc=("s_dv", 1))
                    ps_free[psxn] = [t_ap]
                t_ap_done = pr.tok("s_dv")
                for tc in range(NTC):
                    t_mm2 = E_("tensor", lambda t, tc=tc: [t.matmul(psC[:, h * 512:(h + 1) * 512],
                                                                    apbv(j)[:, tc * 128:(tc + 1) * 128],
                                                                    w8v(16384 + j * 1024 + h * 512, 512),
                                                                    start=(j == 0), stop=(j == 7))
                                                           for h in range(2) for j in range(8)][-1],
                               waits=[t_sc2, t_ap_done] + ps_free["psC"], inc=("s_pe", 1))
                    t_dep = E_("vector", lambda v, tc=tc, e=e:
                               v.scalar_tensor_tensor(yacc[:, tc * DM:(tc + 1) * DM], psC[:],
                                                      comb[:, tc * NE + e:tc * NE + e + 1],
                                                      yacc[:, tc * DM:(tc + 1) * DM],
                                                      op0=ALU.mult, op1=ALU.add),
                               waits=[t_mm2, t_comb_done], inc=("s_dv", 1))
                    ps_free["psC"] = [t_dep]
        t_phF_pe = pr.tok("s_pe")
        t_phF_dv = pr.tok("s_dv")

        if stop_i < PHASES.index("G"):
            continue
        if "shared" in parts:
            tw_s2 = None
            for fsc in range(NEC):
                tw_s2 = E_("sync", lambda s, fsc=fsc: s.dma_start(wB4v(fsc * 1024, DM),
                                                                  sc_sfc2T[fsc * 128:(fsc + 1) * 128, :]),
                           waits=[t_phF_pe], inc=("s_w2", 16))
            t_ssc = None
            for fsc in range(NEC):
                t_ssc = E_("vector", lambda v, fsc=fsc: v.tensor_scalar_mul(wB4v(fsc * 1024, DM),
                                                                            wB4v(fsc * 1024, DM),
                                                                            smult[:, fsc:fsc + 1]),
                           waits=[tw_s2, t_phF_dv], inc=("s_dv", 1))
            slot_tok = [None, None]
            slot_sem_g = ["s_wa", "s_wb"]
            for j in range(NEC):
                sl = j % 2
                base = 16384 + sl * 2048
                t_wj = None
                for dmc in range(NDMC):
                    w0 = ([t_phF_pe if j < 2 else pr.tok("s_pe")] +
                          ([slot_tok[sl]] if slot_tok[sl] else []) +
                          ([pr.tok("s_dv")] if j == 0 else [])) if dmc == 0 else []
                    t_wj = E_("sync", lambda s, j=j, base=base, dmc=dmc:
                              s.dma_start(w8[:, base + dmc * 256: base + dmc * 256 + 256],
                                          sc_sfc1T[dmc * 128:(dmc + 1) * 128, :]
                                          .rearrange("p (s a c) -> p a s c", s=2, c=128)[:, j]),
                              waits=w0, inc=(slot_sem_g[sl], 16))
                psgn, psxn = ("psA", "psB") if j % 2 == 0 else ("psB", "psA")
                psg_, psx_ = PS[psgn], PS[psxn]
                t_mmg = E_("tensor", lambda t, base=base, psg_=psg_: [t.matmul(psg_[:, h * 512:(h + 1) * 512],
                                                                               w8[:, base + dmc * 256 + 128: base + dmc * 256 + 256],
                                                                               xnT[:, dmc * TS + h * 512: dmc * TS + (h + 1) * 512],
                                                                               start=(dmc == 0), stop=(dmc == NDMC - 1))
                                                                      for h in range(2) for dmc in range(NDMC)][-1],
                           waits=[t_wj] + ps_free[psgn], inc=("s_pe", 1))
                t_gl = E_("scalar", lambda sc, psg_=psg_, j=j: sc.activation(stg[:, 0:TS], psg_[:], AF.Gelu,
                                                                             bias=sfc1bs[:, 16 + j:17 + j]),
                          waits=[t_mmg, pr.tok("s_dv")], inc=("s_ac", 1))
                ps_free[psgn] = [t_gl]
                t_mmx = E_("tensor", lambda t, base=base, psx_=psx_: [t.matmul(psx_[:, h * 512:(h + 1) * 512],
                                                                               w8[:, base + dmc * 256: base + dmc * 256 + 128],
                                                                               xnT[:, dmc * TS + h * 512: dmc * TS + (h + 1) * 512],
                                                                               start=(dmc == 0), stop=(dmc == NDMC - 1))
                                                                      for h in range(2) for dmc in range(NDMC)][-1],
                           waits=[t_wj] + ps_free[psxn], inc=("s_pe", 1))
                t_as = E_("vector", lambda v, psx_=psx_, j=j:
                          v.scalar_tensor_tensor(w8[:, j * 1024:(j + 1) * 1024],
                                                 psx_[:], sfc1bs[:, j:j + 1], stg[:, 0:TS],
                                                 op0=ALU.add, op1=ALU.mult),
                          waits=[t_mmx, t_gl], inc=("s_dv", 1))
                slot_tok[sl] = t_mmx
                ps_free[psxn] = [t_as]
            t_as_done = pr.tok("s_dv")
            for tc in range(NTC):
                E_("tensor", lambda t, tc=tc: [t.matmul(psC[:, h * 512:(h + 1) * 512],
                                                        w8[:, j * 1024 + tc * 128: j * 1024 + (tc + 1) * 128],
                                                        wB4v(j * 1024 + h * 512, 512),
                                                        start=(j == 0), stop=False)
                                               for h in range(2) for j in range(NEC)][-1],
                   waits=[t_as_done, t_ssc] + ps_free["psC"], inc=("s_pe", 1))
                if "moe" in parts:
                    E_("tensor", lambda t, tc=tc: [t.matmul(psC[:, h * 512:(h + 1) * 512],
                                                            combT[0:NE, tc * 128:(tc + 1) * 128],
                                                            fc2bias[0:NE, h * 512:(h + 1) * 512],
                                                            start=False, stop=False) for h in range(2)][-1],
                       waits=[t_xnT_done, t_gp_all], inc=("s_pe", 1))
                t_mmb2 = E_("tensor", lambda t, tc=tc: [t.matmul(psC[:, h * 512:(h + 1) * 512],
                                                                 onesb[0:1, 0:128],
                                                                 sfc2bias[0:1, h * 512:(h + 1) * 512],
                                                                 start=False, stop=True) for h in range(2)][-1],
                            waits=[t_gp_all], inc=("s_pe", 1))
                t_dep = E_("vector", lambda v, tc=tc: v.tensor_add(yacc[:, tc * DM:(tc + 1) * DM],
                                                                   yacc[:, tc * DM:(tc + 1) * DM], psC[:]),
                           waits=[t_mmb2], inc=("s_dv", 1))
                ps_free["psC"] = [t_dep]

        if stop_i < PHASES.index("H"):
            continue
        for tc in range(NTC):
            slot = tc % 2
            xv = stg[:, slot * TS: slot * TS + DM]
            if "resid" in parts:
                t_x = E_("sync", lambda s, b=b, tb=tb, tc=tc, xv=xv:
                         s.dma_start(xv, d_x[b, tb + tc * 128: tb + (tc + 1) * 128, :]),
                         waits=[pr.tok("s_dv"), pr.tok("s_ac")], inc=("s_x", 16))
                t_add = E_("vector", lambda v, tc=tc, xv=xv: v.tensor_add(yacc[:, tc * DM:(tc + 1) * DM],
                                                                          yacc[:, tc * DM:(tc + 1) * DM], xv),
                           waits=[t_x], inc=("s_dv", 1))
            else:
                t_add = pr.tok("s_dv")
            E_("sync", lambda s, b=b, tb=tb, tc=tc: s.dma_start(d_out[b, tb + tc * 128: tb + (tc + 1) * 128, :],
                                                                yacc[:, tc * DM:(tc + 1) * DM]),
               waits=[t_add], inc=("s_out", 16))

        prev_w8_free = pr.tok("s_pe")
        prev_wB4_free = pr.tok("s_pe")

    E_("vector", lambda v: None,
       waits=[pr.tok(s) for s in ("s_out", "s_in", "s_w", "s_wt", "s_wa", "s_wb",
                                  "s_w2", "s_sp", "s_rl", "s_bc", "s_gp", "s_ym", "s_x", "s_tap")])

    with stack:
        with nc.Block() as blk:
            OWN = {"vector": "s_dv", "scalar": "s_ac", "gpsimd": "s_gp"}

            def realize(engname):
                own = OWN.get(engname)

                def body(eng):
                    prev = 0
                    for waits, fn, incv in pr.ops[engname]:
                        seen = False
                        for (s, v) in waits:
                            if own is not None and s == own:
                                v = max(v, prev)
                                seen = True
                            eng.wait_ge(sems[s], v)
                        if own is not None and not seen and prev > 0:
                            eng.wait_ge(sems[own], prev)
                        inst = fn(eng) if fn is not None else None
                        if isinstance(inst, list):
                            inst = inst[-1]
                        if incv is not None:
                            sname, delta = incv
                            assert inst is not None, f"inc without instruction on {engname}"
                            inst.then_inc(sems[sname], delta)
                            if sname == own:
                                prev += delta
                return body

            blk.sync(realize("sync"))
            blk.vector(realize("vector"))
            blk.scalar(realize("scalar"))
            blk.tensor(realize("tensor"))
            blk.gpsimd(realize("gpsimd"))
    return nc


def make_in_maps(inputs):
    import ml_dtypes
    x = np.ascontiguousarray(inputs["L0_attn_x"], dtype=np.float32)
    shards = np.split(x, 8, axis=0)
    base = {
        "w_inproj": inputs["L0_attn_in_proj_w"],
        "w_conv": inputs["L0_attn_conv_w"],
        "b_conv": inputs["L0_attn_conv_b"],
        "w_dt": inputs["L0_attn_dt_w"],
        "b_dt": inputs["L0_attn_dt_b"],
        "w_B": inputs["L0_attn_B_proj_w"],
        "w_C": inputs["L0_attn_C_proj_w"],
        "A_log": inputs["L0_attn_A_log"],
        "D_par": inputs["L0_attn_D_param"],
        "w_outp": inputs["L0_attn_out_proj_w"],
        "w_gate": inputs["L0_ffn_gate_w"],
        "w_fc1": inputs["L0_ffn_fc1_w"],
        "b_fc1": inputs["L0_ffn_fc1_b"],
        "g_mult": inputs["L0_ffn_geglu_mult"],
        "w_fc2": inputs["L0_ffn_fc2_w"],
        "b_fc2": inputs["L0_ffn_fc2_b"],
        "w_sfc1": inputs["L0_ffn_s_fc1_w"],
        "b_sfc1": inputs["L0_ffn_s_fc1_b"],
        "s_mult": inputs["L0_ffn_s_geglu_mult"],
        "w_sfc2": inputs["L0_ffn_s_fc2_w"],
        "b_sfc2": inputs["L0_ffn_s_fc2_b"],
    }
    base = {k: np.ascontiguousarray(v, dtype=np.float32) for k, v in base.items()}
    aux = {
        "ident_bf": np.eye(128, dtype=ml_dtypes.bfloat16),
        "ident_f": np.eye(128, dtype=np.float32),
        "ones_bf": np.ones((1, 128), dtype=ml_dtypes.bfloat16),
    }
    return [dict(base, x=s, **aux) for s in shards]


def kernel(**inputs):
    nc = build_core_program()
    in_maps = make_in_maps(inputs)
    res = run_bass_kernel_spmd(nc, in_maps, core_ids=list(range(8)))
    return np.concatenate([r["out"] for r in res.results], axis=0)


# version 02
# speedup vs baseline: 1.0484x; 1.0484x over previous
import numpy as np
import concourse.bass as bass
import concourse.mybir as mybir
from concourse.bass_utils import run_bass_kernel_spmd

F32 = mybir.dt.float32
BF16 = mybir.dt.bfloat16
AF = mybir.ActivationFunctionType
ALU = mybir.AluOpType
AX = mybir.AxisListType

DM = 1024
E2 = 4096
E = 2048
NST = 16
NE = 15
FF = 1024
FS = 2048
BLOC = 4
L = 2048
TS = 1024
NSB = BLOC * L // TS
NTC = TS // 128
NEC = E // 128
NDMC = DM // 128
ECG = 2
NG = NEC // ECG
EPS = 1e-5

ENGINES = ("sync", "vector", "scalar", "tensor", "gpsimd")


class Prog:
    def __init__(self):
        self.ops = {e: [] for e in ENGINES}
        self.cnt = {}

    def tok(self, sem):
        return (sem, self.cnt.get(sem, 0))

    def emit(self, eng, fn=None, waits=(), inc=None):
        tok = None
        incv = None
        if inc is not None:
            sem, d = inc
            self.cnt[sem] = self.cnt.get(sem, 0) + d
            tok = (sem, self.cnt[sem])
            incv = (sem, d)
        ws = tuple(w for w in waits if w is not None and w[1] > 0)
        self.ops[eng].append((ws, fn, incv))
        return tok


PHASES = ("p0", "A", "B", "Ca", "C", "D", "E", "F", "G", "H")


def build_core_program(parts=("attn", "moe", "shared", "resid"), debug=False,
                       nsb=NSB, stop="H", ca_mode=3):
    stop_i = PHASES.index(stop)
    nc = bass.Bass("TRN2", target_bir_lowering=False)
    pr = Prog()

    def din(name, shape):
        return nc.dram_tensor(name, list(shape), F32, kind="ExternalInput").ap()

    d_x = din("x", (BLOC, L, DM))
    d_inprojT = din("w_inprojT", (DM, E2))
    d_dtwT = din("w_dtT", (E, E))
    d_outpT = din("w_outpT", (E, DM))
    d_sfc1T = din("w_sfc1T", (DM, 2 * FS))
    d_sfc2T = din("w_sfc2T", (FS, DM))
    d_fc1P = din("w_fc1P", (NE, DM, 2 * FF))
    d_fc2T = din("w_fc2T", (NE, FF, DM))
    d_gateT = din("w_gateT", (128, NDMC * NE))
    d_bcT = din("w_bcT", (128, NEC * 2 * NST))
    d_Alog2 = din("A_log2", (128, NEC * NST))
    d_convw2 = din("w_conv2", (128, NEC * 4))
    d_convb2 = din("b_conv2", (128, NEC))
    d_dtb2 = din("b_dt2", (128, NEC))
    d_D2 = din("D_par2", (128, NEC))
    d_sfc1b2 = din("b_sfc1_2", (128, 32))
    d_smult2 = din("s_mult2", (128, NST))
    d_gmult2 = din("g_mult2", (128, NE * 8))
    d_fc1b2 = din("b_fc1_2", (128, NE * 16))
    d_fc2b = din("b_fc2", (NE, DM))
    d_sfc2b = din("b_sfc2", (DM,))
    d_ident = nc.dram_tensor("ident_bf", [128, 128], BF16, kind="ExternalInput").ap()
    d_ones = nc.dram_tensor("ones_bf", [1, 128], BF16, kind="ExternalInput").ap()

    d_out = nc.dram_tensor("out", [BLOC, L, DM], F32, kind="ExternalOutput").ap()

    def dint(name, shape, dt):
        return nc.dram_tensor(name, list(shape), dt, kind="Internal").ap()

    sc_inprojT = dint("sc_inprojT", (DM, E2), BF16)
    sc_dtwT = dint("sc_dtwT", (E, E), BF16)
    sc_outpT = dint("sc_outpT", (E, DM), BF16)
    sc_fc1T = dint("sc_fc1T", (NE, DM, 2 * FF), BF16)
    sc_fc2T = dint("sc_fc2T", (NE, FF, DM), BF16)
    sc_sfc1T = dint("sc_sfc1T", (DM, 2 * FS), BF16)
    sc_sfc2T = dint("sc_sfc2T", (FS, DM), BF16)
    sp_dt = dint("sp_dt", (NEC, 128, TS), F32)
    sp_xs = dint("sp_xs", (NEC, 128, TS), F32)
    sp_z = dint("sp_z", (NEC, 128, TS), F32)
    sp_ym = dint("sp_ym", (NEC, 128, TS), BF16)
    sc_bc = dint("sc_bc", (2 * NST, TS), F32)

    kind_dbg = "ExternalOutput" if debug else "Internal"
    d_tap_comb = nc.dram_tensor("tap_comb", [128, NTC * NE], F32, kind=kind_dbg).ap()
    d_tap_bc = nc.dram_tensor("tap_bc", [2 * NST, TS], F32, kind=kind_dbg).ap()
    d_tap_att = nc.dram_tensor("tap_att", [128, NTC * DM], F32, kind=kind_dbg).ap()
    d_tap_xnT = nc.dram_tensor("tap_xnT", [128, NDMC * TS], BF16, kind=kind_dbg).ap()
    d_tap_lnst = nc.dram_tensor("tap_lnst", [128, 32], F32, kind=kind_dbg).ap()

    import contextlib
    stack = contextlib.ExitStack()

    def sbuf(name, shape, dt=F32):
        return stack.enter_context(nc.sbuf_tensor(name, list(shape), dt))

    w8 = sbuf("w8", (128, 24576), BF16)
    wB4 = sbuf("wB4", (128, 16384), BF16)
    xnT = sbuf("xnT", (128, NDMC * TS), BF16)
    yacc = sbuf("yacc", (128, NTC * DM), F32)
    trio = sbuf("trio", (128, 12288), BF16)
    ymst = sbuf("ymst", (128, 2 * TS), BF16)
    stg = sbuf("stg", (128, 2 * TS), F32)
    rlbuf = sbuf("rlbuf", (128, 2 * TS), F32)
    bcrep = sbuf("bcrep", (128, 2 * TS), F32)
    tmpa = sbuf("tmpa", (128, TS), F32)
    tmpb = sbuf("tmpb", (128, TS), F32)
    tmph = sbuf("tmph", (128, TS), F32)
    combT = sbuf("combT", (NST, TS), BF16)
    idb = sbuf("idb", (128, 128), BF16)
    gateT = sbuf("gateT", (128, NDMC * NE), BF16)
    bcT = sbuf("bcT", (128, NEC * 2 * NST), BF16)
    fc2bias = sbuf("fc2bias", (NE, DM), BF16)
    sfc2bias = sbuf("sfc2bias", (1, DM), BF16)
    onesb = sbuf("onesb", (1, 128), BF16)
    combb = sbuf("combb", (128, NE), BF16)
    xpadv = rlbuf[:, 0:TS + 3]
    sqtv = rlbuf[:, TS:2 * TS]
    tails = sbuf("tails", (128, NEC * 3), F32)
    hstate = sbuf("hstate", (128, NEC * NST), F32)
    A_sb = sbuf("A_sb", (128, NEC * NST), F32)
    convw = sbuf("convw", (128, NEC * 4), F32)
    convb = sbuf("convb", (128, NEC), F32)
    dtb = sbuf("dtb", (128, NEC), F32)
    D_sb = sbuf("D_sb", (128, NEC), F32)
    gmult = sbuf("gmult", (128, NE * 8), F32)
    fc1bs = sbuf("fc1bs", (128, NE * 16), F32)
    sfc1bs = sbuf("sfc1bs", (128, 32), F32)
    smult = sbuf("smult", (128, NST), F32)
    comb = sbuf("comb", (128, NTC * NE), F32)
    lnst = sbuf("lnst", (128, 32), F32)

    trioF = trio[:].bitcast(F32)

    def dtg(i):
        return trioF[:, i * TS:(i + 1) * TS]

    def ug(i):
        return trioF[:, (ECG + i) * TS:(ECG + i + 1) * TS]

    def ysg(i):
        return trioF[:, (2 * ECG + i) * TS:(2 * ECG + i + 1) * TS]

    def apbv(j):
        return trio[:, j * TS:(j + 1) * TS]

    psA = stack.enter_context(nc.psum_tensor("psA", [128, 1024], F32))
    psB = stack.enter_context(nc.psum_tensor("psB", [128, 1024], F32))
    psC = stack.enter_context(nc.psum_tensor("psC", [128, 1024], F32))
    psT = stack.enter_context(nc.psum_tensor("psT", [128, 1024], BF16))
    psG = stack.enter_context(nc.psum_tensor("psG", [128, 512], F32))
    PS = {"psA": psA, "psB": psB, "psC": psC}

    sems = {}
    for s in ["s_in", "s_w", "s_sp", "s_rl", "s_bc", "s_out", "s_pe", "s_dv",
              "s_ac", "s_gp", "s_ym", "s_x", "s_wt", "s_wa", "s_wb", "s_w2", "s_tap"]:
        sems[s] = stack.enter_context(nc.semaphore(s))

    E_ = pr.emit
    ps_free = {"psA": [], "psB": [], "psC": [], "psT": [], "psG": []}

    ct = {}
    ct["inproj"] = E_("gpsimd", lambda g: g.dma_start(sc_inprojT, d_inprojT), inc=("s_gp", 16))
    ct["dtw"] = E_("gpsimd", lambda g: g.dma_start(sc_dtwT, d_dtwT), inc=("s_gp", 16))
    for e in range(NE):
        ct[("fc1", e)] = E_("gpsimd", lambda g, e=e: g.dma_start(sc_fc1T[e], d_fc1P[e]), inc=("s_gp", 16))
        ct[("fc2", e)] = E_("gpsimd", lambda g, e=e: g.dma_start(sc_fc2T[e], d_fc2T[e]), inc=("s_gp", 16))
    ct["outp"] = E_("gpsimd", lambda g: g.dma_start(sc_outpT, d_outpT), inc=("s_gp", 16))
    ct["sfc1"] = E_("gpsimd", lambda g: g.dma_start(sc_sfc1T, d_sfc1T), inc=("s_gp", 16))
    ct["sfc2"] = E_("gpsimd", lambda g: g.dma_start(sc_sfc2T, d_sfc2T), inc=("s_gp", 16))
    E_("gpsimd", lambda g: g.dma_start(fc2bias[:], d_fc2b), inc=("s_gp", 16))
    E_("gpsimd", lambda g: g.dma_start(sfc2bias[:], d_sfc2b.rearrange("(o d) -> o d", o=1)), inc=("s_gp", 16))
    E_("gpsimd", lambda g: g.dma_start(gateT[:], d_gateT), inc=("s_gp", 16))
    E_("gpsimd", lambda g: g.dma_start(bcT[:], d_bcT), inc=("s_gp", 16))
    t_gp_all = pr.tok("s_gp")

    E_("sync", lambda s: s.dma_start(idb[:], d_ident), inc=("s_in", 16))
    E_("sync", lambda s: s.dma_start(onesb[:], d_ones), inc=("s_in", 16))
    E_("sync", lambda s: s.dma_start(A_sb[:], d_Alog2), inc=("s_in", 16))
    E_("sync", lambda s: s.dma_start(convw[:], d_convw2), inc=("s_in", 16))
    E_("sync", lambda s: s.dma_start(convb[:], d_convb2), inc=("s_in", 16))
    E_("sync", lambda s: s.dma_start(dtb[:], d_dtb2), inc=("s_in", 16))
    E_("sync", lambda s: s.dma_start(D_sb[:], d_D2), inc=("s_in", 16))
    E_("sync", lambda s: s.dma_start(sfc1bs[:], d_sfc1b2), inc=("s_in", 16))
    E_("sync", lambda s: s.dma_start(smult[:], d_smult2), inc=("s_in", 16))
    E_("sync", lambda s: s.dma_start(gmult[:], d_gmult2), inc=("s_in", 16))
    t_A_ld = E_("sync", lambda s: s.dma_start(fc1bs[:], d_fc1b2), inc=("s_in", 16))

    t_Ae = E_("scalar", lambda sc: sc.activation(A_sb[:], A_sb[:], AF.Exp),
              waits=[pr.tok("s_in")], inc=("s_ac", 1))
    E_("vector", lambda v: v.tensor_scalar_mul(A_sb[:], A_sb[:], -1.0),
       waits=[t_Ae], inc=("s_dv", 1))

    t_p0_ac = pr.tok("s_ac")
    t_p0_in = pr.tok("s_in")
    E_("vector", lambda v: v.memset(hstate[:], 0.0), inc=("s_dv", 1))
    E_("vector", lambda v: v.memset(tails[:], 0.0), inc=("s_dv", 1))
    t_p0_dv = pr.tok("s_dv")

    def w8v(c0, w):
        return w8[:, c0:c0 + w]

    def wB4v(c0, w):
        return wB4[:, c0:c0 + w]

    xsbv = wB4

    prev_w8_free = None
    prev_wB4_free = None

    for sb_i in range(nsb):
        b = sb_i // 2
        half = sb_i % 2
        tb = half * TS
        first = sb_i == 0

        w_wait = [ct["inproj"]] if first else [prev_w8_free]
        t_w_inproj = None
        for dmc in range(NDMC):
            t_w_inproj = E_("sync", lambda s, dmc=dmc: s.dma_start(w8v(dmc * 2048, 2048),
                                                                   sc_inprojT[dmc * 128:(dmc + 1) * 128, 0:2048]),
                            waits=w_wait, inc=("s_w", 16))

        if stop_i < PHASES.index("A"):
            continue
        waitsA = [t_p0_dv, t_p0_in, t_gp_all, t_p0_ac] if first else []
        for tc in range(NTC):
            xv = stg[:, 0:DM]
            t_x = E_("sync", lambda s, b=b, tb=tb, tc=tc, xv=xv:
                     s.dma_start(xv, d_x[b, tb + tc * 128: tb + (tc + 1) * 128, :]),
                     waits=waitsA + [pr.tok("s_dv"), pr.tok("s_ac"), pr.tok("s_tap")], inc=("s_x", 16))
            c0 = (tc % 2) * 8
            t1 = E_("vector", lambda v, xv=xv, c0=c0: v.reduce_sum(lnst[:, c0:c0 + 1], xv, axis=AX.X),
                    waits=[t_x], inc=("s_dv", 1))
            t2 = E_("vector", lambda v, c0=c0: v.tensor_scalar_mul(lnst[:, c0 + 1:c0 + 2], lnst[:, c0:c0 + 1], 1.0 / DM),
                    waits=[t1], inc=("s_dv", 1))
            t3 = E_("vector", lambda v, xv=xv, c0=c0: v.tensor_single_scalar(sqtv, xv, lnst[:, c0 + 1:c0 + 2],
                                                                             op=ALU.subtract),
                    waits=[t2, pr.tok("s_ac"), pr.tok("s_rl")], inc=("s_dv", 1))
            t4 = E_("scalar", lambda sc: sc.activation(sqtv, sqtv, AF.Square), waits=[t3], inc=("s_ac", 1))
            t5 = E_("vector", lambda v, c0=c0: v.reduce_sum(lnst[:, c0 + 2:c0 + 3], sqtv, axis=AX.X),
                    waits=[t4], inc=("s_dv", 1))
            t5b = E_("vector", lambda v, c0=c0: v.tensor_scalar(lnst[:, c0 + 2:c0 + 3], lnst[:, c0 + 2:c0 + 3],
                                                                1.0 / DM, EPS, op0=ALU.mult, op1=ALU.add),
                     waits=[t5], inc=("s_dv", 1))
            t6 = E_("scalar", lambda sc, c0=c0: sc.activation(lnst[:, c0 + 3:c0 + 4], lnst[:, c0 + 2:c0 + 3],
                                                              AF.Ln),
                    waits=[t5b], inc=("s_ac", 1))
            t7 = E_("vector", lambda v, c0=c0: v.tensor_scalar_mul(lnst[:, c0 + 4:c0 + 5], lnst[:, c0 + 3:c0 + 4], -0.5),
                    waits=[t6], inc=("s_dv", 1))
            t8 = E_("scalar", lambda sc, c0=c0: sc.activation(lnst[:, c0 + 5:c0 + 6], lnst[:, c0 + 4:c0 + 5], AF.Exp),
                    waits=[t7], inc=("s_ac", 1))
            t9 = E_("vector", lambda v, xv=xv, c0=c0:
                    v.tensor_scalar(sqtv, xv, lnst[:, c0 + 1:c0 + 2], lnst[:, c0 + 5:c0 + 6],
                                    op0=ALU.subtract, op1=ALU.mult),
                    waits=[t8], inc=("s_dv", 1))
            t10 = E_("vector", lambda v: v.tensor_copy(ymst[:, 0:DM], sqtv),
                     waits=[pr.tok("s_pe")], inc=("s_dv", 1))
            t_tr = E_("tensor", lambda t: [t.transpose(psT[:, dmc * 128:(dmc + 1) * 128],
                                                       ymst[:, dmc * 128:(dmc + 1) * 128], idb[:])
                                           for dmc in range(NDMC)][-1],
                      waits=[t10] + ps_free["psT"], inc=("s_pe", 1))
            t_cp = E_("scalar", lambda sc, tc=tc:
                      sc.copy(xnT.rearrange("p (a t) -> p a t", a=NDMC)[:, :, tc * 128:(tc + 1) * 128],
                              psT.rearrange("p (a t) -> p a t", a=NDMC)[:, :, 0:128]),
                      waits=[t_tr], inc=("s_ac", 1))
            ps_free["psT"] = [t_cp]

            t_g = E_("tensor", lambda t, tc=tc: [t.matmul(psG[:, 0:NE],
                                                          xnT[:, dmc * TS + tc * 128: dmc * TS + (tc + 1) * 128],
                                                          gateT[:, dmc * NE:(dmc + 1) * NE],
                                                          start=(dmc == 0), stop=(dmc == NDMC - 1))
                                                 for dmc in range(NDMC)][-1],
                     waits=[t_cp] + ps_free["psG"], inc=("s_pe", 1))
            cc = tc * NE
            t_sm1 = E_("vector", lambda v: v.reduce_max(lnst[:, 16:17], psG[:, 0:NE], axis=AX.X),
                       waits=[t_g, pr.tok("s_ac")], inc=("s_dv", 1))
            t_negm = E_("vector", lambda v: v.tensor_scalar_mul(lnst[:, 17:18], lnst[:, 16:17], -1.0),
                        waits=[t_sm1], inc=("s_dv", 1))
            t_sm2 = E_("scalar", lambda sc, cc=cc: sc.activation(comb[:, cc:cc + NE], psG[:, 0:NE], AF.Exp,
                                                                 bias=lnst[:, 17:18]),
                       waits=[t_negm], inc=("s_ac", 1))
            t_sm3 = E_("vector", lambda v, cc=cc: v.reduce_sum(lnst[:, 18:19], comb[:, cc:cc + NE], axis=AX.X),
                       waits=[t_sm2], inc=("s_dv", 1))
            t_rc = E_("vector", lambda v: v.reciprocal(lnst[:, 19:20], lnst[:, 18:19]),
                      waits=[t_sm3], inc=("s_dv", 1))
            t_scl = E_("vector", lambda v, cc=cc: v.tensor_single_scalar(comb[:, cc:cc + NE], comb[:, cc:cc + NE],
                                                                         lnst[:, 19:20], op=ALU.mult),
                       waits=[t_rc], inc=("s_dv", 1))
            ps_free["psG"] = [t_sm3, t_sm2]
            t_k0 = E_("vector", lambda v, cc=cc: v.tensor_copy(sqtv[:, 0:NE], comb[:, cc:cc + NE]),
                      waits=[t_scl], inc=("s_dv", 1))
            t_k1 = E_("vector", lambda v: v.memset(sqtv[:, 16:16 + NE], 0.0), inc=("s_dv", 1))
            t_k2 = E_("vector", lambda v: v.memset(lnst[:, 20:21], 0.0), inc=("s_dv", 1))
            t_prev = t_k2
            for r in range(3):
                t_rm = E_("vector", lambda v: v.reduce_max(lnst[:, 21:22], sqtv[:, 0:NE], axis=AX.X),
                          waits=[t_prev], inc=("s_dv", 1))
                t_eq = E_("vector", lambda v: v.tensor_single_scalar(sqtv[:, 32:32 + NE], sqtv[:, 0:NE],
                                                                     lnst[:, 21:22], op=ALU.is_equal),
                          waits=[t_rm], inc=("s_dv", 1))
                E_("vector", lambda v: v.scalar_tensor_tensor(sqtv[:, 16:16 + NE], sqtv[:, 32:32 + NE],
                                                              lnst[:, 21:22], sqtv[:, 16:16 + NE],
                                                              op0=ALU.mult, op1=ALU.add),
                   waits=[t_eq], inc=("s_dv", 1))
                E_("vector", lambda v: v.tensor_add(lnst[:, 20:21], lnst[:, 20:21], lnst[:, 21:22]),
                   waits=[t_rm], inc=("s_dv", 1))
                t_prev = E_("vector", lambda v: v.scalar_tensor_tensor(sqtv[:, 0:NE], sqtv[:, 32:32 + NE], -1e30,
                                                                       sqtv[:, 0:NE], op0=ALU.mult, op1=ALU.add),
                            waits=[t_eq], inc=("s_dv", 1))
            t_dn = E_("vector", lambda v: v.tensor_scalar_add(lnst[:, 20:21], lnst[:, 20:21], 1e-20),
                      waits=[t_prev], inc=("s_dv", 1))
            t_rc2 = E_("vector", lambda v: v.reciprocal(lnst[:, 22:23], lnst[:, 20:21]),
                       waits=[t_dn], inc=("s_dv", 1))
            t_fin = E_("vector", lambda v, cc=cc: v.tensor_single_scalar(comb[:, cc:cc + NE], sqtv[:, 16:16 + NE],
                                                                         lnst[:, 22:23], op=ALU.mult),
                       waits=[t_rc2], inc=("s_dv", 1))
            t_cb = E_("vector", lambda v, cc=cc: v.tensor_copy(combb[:], comb[:, cc:cc + NE]),
                      waits=[t_fin, pr.tok("s_pe")], inc=("s_dv", 1))
            t_ctr = E_("tensor", lambda t: t.transpose(psT[0:NE, 0:128], combb[:], idb[:]),
                       waits=[t_cb] + ps_free["psT"], inc=("s_pe", 1))
            t_ctc = E_("scalar", lambda sc, tc=tc: sc.copy(combT[0:NE, tc * 128:(tc + 1) * 128], psT[0:NE, 0:128]),
                       waits=[t_ctr], inc=("s_ac", 1))
            ps_free["psT"] = [t_ctc]
        t_comb_done = pr.tok("s_dv")
        t_xnT_done = pr.tok("s_ac")

        if debug and first:
            E_("sync", lambda s: s.dma_start(d_tap_comb, comb[:]), waits=[t_comb_done], inc=("s_tap", 16))
            E_("sync", lambda s: s.dma_start(d_tap_xnT, xnT[:]), waits=[t_xnT_done], inc=("s_tap", 16))

        if stop_i < PHASES.index("B"):
            continue
        for m in range(2 * NEC):
            if m == NEC:
                t_half_pe = pr.tok("s_pe")
                for dmc in range(NDMC):
                    t_w_inproj = E_("sync", lambda s, dmc=dmc:
                                    s.dma_start(w8v(dmc * 2048, 2048),
                                                sc_inprojT[dmc * 128:(dmc + 1) * 128, 2048:4096]),
                                    waits=[t_half_pe], inc=("s_w", 16))
            psn = "psA" if m % 2 == 0 else "psB"
            ps = PS[psn]
            ml = m % NEC
            t_mm = E_("tensor", lambda t, ml=ml, ps=ps: [t.matmul(ps[:, h * 512:(h + 1) * 512],
                                                                  w8v(dmc * 2048 + ml * 128, 128),
                                                                  xnT[:, dmc * TS + h * 512: dmc * TS + (h + 1) * 512],
                                                                  start=(dmc == 0), stop=(dmc == NDMC - 1))
                                                         for h in range(2) for dmc in range(NDMC)][-1],
                      waits=[t_w_inproj, t_xnT_done] + ps_free[psn], inc=("s_pe", 1))
            slot = m % 2
            sv = stg[:, slot * TS:(slot + 1) * TS]
            if m < NEC:
                ec = m
                if half == 0:
                    t_ti = E_("vector", lambda v: v.memset(xpadv[:, 0:3], 0.0),
                              waits=[pr.tok("s_ac"), pr.tok("s_rl")], inc=("s_dv", 1))
                else:
                    t_ti = E_("vector", lambda v, ec=ec: v.tensor_copy(xpadv[:, 0:3], tails[:, ec * 3:(ec + 1) * 3]),
                              waits=[pr.tok("s_ac"), pr.tok("s_rl")], inc=("s_dv", 1))
                t_ev = E_("scalar", lambda sc, ps=ps: sc.copy(xpadv[:, 3:3 + TS], ps[:]),
                          waits=[t_mm, t_ti, pr.tok("s_dv")], inc=("s_ac", 1))
                ps_free[psn] = [t_ev]
                E_("vector", lambda v, ec=ec: v.tensor_copy(tails[:, ec * 3:(ec + 1) * 3], xpadv[:, TS:TS + 3]),
                   waits=[t_ev], inc=("s_dv", 1))
                t_c = E_("vector", lambda v, ec=ec: v.tensor_scalar_mul(tmpb[:], xpadv[:, 0:TS],
                                                                        convw[:, ec * 4:ec * 4 + 1]),
                         inc=("s_dv", 1))
                for k in range(1, 4):
                    t_c = E_("vector", lambda v, ec=ec, k=k: v.scalar_tensor_tensor(tmpb[:], xpadv[:, k:k + TS],
                                                                                    convw[:, ec * 4 + k:ec * 4 + k + 1],
                                                                                    tmpb[:], op0=ALU.mult, op1=ALU.add),
                             inc=("s_dv", 1))
                t_sl = E_("scalar", lambda sc, sv=sv, ec=ec: sc.activation(sv, tmpb[:], AF.Silu,
                                                                           bias=convb[:, ec:ec + 1]),
                          waits=[t_c, pr.tok("s_sp"), pr.tok("s_x"), pr.tok("s_dv")], inc=("s_ac", 1))
                E_("sync", lambda s, sv=sv, ec=ec: s.dma_start(sp_xs[ec], sv), waits=[t_sl], inc=("s_sp", 16))
                E_("vector", lambda v, sv=sv, ec=ec: v.tensor_copy(xsbv[:, ec * TS:(ec + 1) * TS], sv),
                   waits=[t_sl] + ([prev_wB4_free] if (ec == 0 and prev_wB4_free) else []), inc=("s_dv", 1))
            else:
                ec = m - NEC
                t_sl = E_("scalar", lambda sc, sv=sv, ps=ps: sc.activation(sv, ps[:], AF.Silu),
                          waits=[t_mm, pr.tok("s_sp"), pr.tok("s_x"), pr.tok("s_dv")], inc=("s_ac", 1))
                ps_free[psn] = [t_sl]
                E_("sync", lambda s, sv=sv, ec=ec: s.dma_start(sp_z[ec], sv), waits=[t_sl], inc=("s_sp", 16))
        t_phB_sp = pr.tok("s_sp")
        t_phB_dv = pr.tok("s_dv")

        t_last_inproj_mm = pr.tok("s_pe")
        t_w_dt = None
        for ec in range(NEC):
            t_w_dt = E_("sync", lambda s, ec=ec: s.dma_start(w8v(ec * 1024, 1024),
                                                             sc_dtwT[ec * 128:(ec + 1) * 128, 0:1024]),
                        waits=[t_last_inproj_mm] + ([ct["dtw"]] if first else []), inc=("s_w", 16))

        if stop_i < PHASES.index("Ca"):
            continue
        for m in range(NEC):
            if m == NEC // 2:
                t_half_pe = pr.tok("s_pe")
                for ec in range(NEC):
                    t_w_dt = E_("sync", lambda s, ec=ec:
                                s.dma_start(w8v(ec * 1024, 1024),
                                            sc_dtwT[ec * 128:(ec + 1) * 128, 1024:2048]),
                                waits=[t_half_pe], inc=("s_w", 16))
            psn = "psA" if m % 2 == 0 else "psB"
            ps = PS[psn]
            ml = m % (NEC // 2)
            t_mm = E_("tensor", lambda t, ml=ml, ps=ps: [t.matmul(ps[:, h * 512:(h + 1) * 512],
                                                                  w8v(ec * 1024 + ml * 128, 128),
                                                                  xsbv[:, ec * TS + h * 512: ec * TS + (h + 1) * 512],
                                                                  start=(ec == 0), stop=(ec == NEC - 1))
                                                         for h in range(2) for ec in range(NEC)][-1],
                      waits=[t_w_dt, t_phB_dv] + ps_free[psn], inc=("s_pe", 1))
            if ca_mode >= 2:
                t_a1 = E_("scalar", lambda sc, ps=ps, m=m: sc.activation(tmpa[:], ps[:], AF.Abs, bias=dtb[:, m:m + 1]),
                          waits=[t_mm, pr.tok("s_dv")], inc=("s_ac", 1))
                E_("scalar", lambda sc: sc.activation(tmpa[:], tmpa[:], AF.Exp, scale=-1.0), inc=("s_ac", 1))
                t_a3 = E_("scalar", lambda sc: sc.activation(tmpa[:], tmpa[:], AF.Ln, bias=1.0), inc=("s_ac", 1))
            else:
                t_a1 = E_("scalar", lambda sc, ps=ps, m=m: sc.copy(tmpa[:], ps[:]),
                          waits=[t_mm, pr.tok("s_dv")], inc=("s_ac", 1))
                t_a3 = t_a1
            slot = m % 2
            sv = stg[:, slot * TS:(slot + 1) * TS]
            if ca_mode >= 1:
                t_mx = E_("vector", lambda v, ps=ps, m=m, sv=sv: v.tensor_scalar(sv, ps[:], dtb[:, m:m + 1], 0.0,
                                                                                 op0=ALU.add, op1=ALU.max),
                          waits=[t_mm, t_a1, pr.tok("s_sp")], inc=("s_dv", 1))
            else:
                t_mx = E_("vector", lambda v, sv=sv: v.tensor_copy(sv, tmpb[:]),
                          waits=[t_mm, pr.tok("s_sp"), pr.tok("s_ac")], inc=("s_dv", 1))
            ps_free[psn] = [t_mx, t_a1]
            t_dt = E_("vector", lambda v, sv=sv: v.tensor_add(sv, sv, tmpa[:]), waits=[t_a3], inc=("s_dv", 1))
            E_("sync", lambda s, sv=sv, m=m: s.dma_start(sp_dt[m], sv), waits=[t_dt], inc=("s_sp", 16))
        if stop_i < PHASES.index("C"):
            continue
        t_bc_mm = E_("tensor", lambda t: [t.matmul(psC[0:32, h * 512:(h + 1) * 512],
                                                   bcT[:, ec * 32:(ec + 1) * 32],
                                                   xsbv[:, ec * TS + h * 512: ec * TS + (h + 1) * 512],
                                                   start=(ec == 0), stop=(ec == NEC - 1))
                                          for h in range(2) for ec in range(NEC)][-1],
                     waits=[t_phB_dv, t_p0_ac] + ([t_gp_all] if first else []) + ps_free["psC"], inc=("s_pe", 1))
        t_bc_ev = E_("scalar", lambda sc: sc.copy(stg[0:32, 0:TS], psC[0:32, :]),
                     waits=[t_bc_mm, pr.tok("s_sp")], inc=("s_ac", 1))
        ps_free["psC"] = [t_bc_ev]
        t_bc_st = E_("sync", lambda s: s.dma_start(sc_bc, stg[0:32, 0:TS]), waits=[t_bc_ev], inc=("s_sp", 16))
        t_phC_sp = pr.tok("s_sp")

        if debug and first:
            E_("sync", lambda s: s.dma_start(d_tap_bc, stg[0:32, 0:TS]), waits=[t_bc_ev], inc=("s_tap", 16))

        t_phC_pe = pr.tok("s_pe")
        t_w_outp = None
        for ec in range(NEC):
            t_w_outp = E_("sync", lambda s, ec=ec: s.dma_start(wB4v(ec * 1024, DM),
                                                               sc_outpT[ec * 128:(ec + 1) * 128, :]),
                          waits=[t_phC_pe] + ([ct["outp"]] if first else []), inc=("s_w", 16))

        if stop_i < PHASES.index("D"):
            continue
        if half == 0:
            E_("vector", lambda v: v.memset(hstate[:], 0.0), inc=("s_dv", 1))
        for g in range(NG):
            t_rl_grp = None
            for i in range(ECG):
                ec = g * ECG + i
                E_("sync", lambda s, ec=ec, i=i: s.dma_start(dtg(i), sp_dt[ec]),
                   waits=[t_phC_sp, pr.tok("s_dv"), pr.tok("s_pe")], inc=("s_rl", 16))
                t_rl_grp = E_("sync", lambda s, ec=ec, i=i: s.dma_start(rlbuf[:, i * TS:(i + 1) * TS], sp_xs[ec]),
                              waits=[t_phB_sp, pr.tok("s_dv"), pr.tok("s_ac")], inc=("s_rl", 16))
            rl_toks = []
            for i in range(ECG):
                t_u = E_("vector", lambda v, i=i: v.tensor_mul(ug(i), dtg(i), rlbuf[:, i * TS:(i + 1) * TS]),
                         waits=[t_rl_grp], inc=("s_dv", 1))
                rl_toks.append(t_u)
            for n in range(NST):
                E_("sync", lambda s, n=n: s.dma_start(bcrep[:, 0:TS],
                                                      sc_bc[n:n + 1, :].broadcast_to([128, TS])),
                   waits=[t_bc_st, pr.tok("s_dv")], inc=("s_bc", 16))
                t_bC = E_("sync", lambda s, n=n: s.dma_start(bcrep[:, TS:2 * TS],
                                                             sc_bc[NST + n:NST + n + 1, :].broadcast_to([128, TS])),
                          inc=("s_bc", 16))
                for i in range(ECG):
                    ec = g * ECG + i
                    col = ec * NST + n
                    t_aa = E_("scalar", lambda sc, i=i, col=col: sc.activation(tmpa[:], dtg(i), AF.Exp,
                                                                               scale=A_sb[:, col:col + 1]),
                              waits=[rl_toks[i], pr.tok("s_dv")], inc=("s_ac", 1))
                    E_("vector", lambda v, i=i: v.tensor_mul(tmpb[:], ug(i), bcrep[:, 0:TS]),
                       waits=[t_bC], inc=("s_dv", 1))
                    t_scn = E_("vector", lambda v, col=col: v.tensor_tensor_scan(tmph[:], tmpa[:], tmpb[:],
                                                                                 hstate[:, col:col + 1],
                                                                                 op0=ALU.mult, op1=ALU.add),
                               waits=[t_aa], inc=("s_dv", 1))
                    E_("vector", lambda v, col=col: v.tensor_copy(hstate[:, col:col + 1], tmph[:, TS - 1:TS]),
                       waits=[t_scn], inc=("s_dv", 1))
                    if n == 0:
                        E_("vector", lambda v, i=i: v.tensor_mul(ysg(i), tmph[:], bcrep[:, TS:2 * TS]),
                           waits=[t_bC], inc=("s_dv", 1))
                    else:
                        E_("vector", lambda v: v.tensor_mul(tmpa[:], tmph[:], bcrep[:, TS:2 * TS]),
                           waits=[t_bC], inc=("s_dv", 1))
                        E_("vector", lambda v, i=i: v.tensor_add(ysg(i), ysg(i), tmpa[:]), inc=("s_dv", 1))
            t_rl2 = None
            for i in range(ECG):
                ec = g * ECG + i
                E_("sync", lambda s, ec=ec, i=i: s.dma_start(rlbuf[:, i * TS:(i + 1) * TS], sp_xs[ec]),
                   waits=[pr.tok("s_dv")], inc=("s_rl", 16))
                t_rl2 = E_("sync", lambda s, ec=ec, i=i: s.dma_start(stg[:, i * TS:(i + 1) * TS], sp_z[ec]),
                           waits=[t_phC_sp, pr.tok("s_ac"), pr.tok("s_dv"), pr.tok("s_tap")], inc=("s_rl", 16))
            for i in range(ECG):
                ec = g * ECG + i
                t_y2 = E_("vector", lambda v, ec=ec, i=i:
                          v.scalar_tensor_tensor(ysg(i), rlbuf[:, i * TS:(i + 1) * TS],
                                                 D_sb[:, ec:ec + 1], ysg(i), op0=ALU.mult, op1=ALU.add),
                          waits=[t_rl2], inc=("s_dv", 1))
                slot = ec % 2
                t_ym = E_("vector", lambda v, i=i, slot=slot:
                          v.tensor_mul(ymst[:, slot * TS:(slot + 1) * TS], ysg(i), stg[:, i * TS:(i + 1) * TS]),
                          waits=[t_rl2, pr.tok("s_ym")], inc=("s_dv", 1))
                E_("sync", lambda s, ec=ec, slot=slot: s.dma_start(sp_ym[ec], ymst[:, slot * TS:(slot + 1) * TS]),
                   waits=[t_ym], inc=("s_ym", 16))
        t_ym_all = pr.tok("s_ym")

        if stop_i < PHASES.index("E"):
            continue
        for tc in range(NTC):
            t_yl = None
            for ec in range(NEC):
                t_yl = E_("sync", lambda s, ec=ec, tc=tc: s.dma_start(ymst[:, ec * 128:(ec + 1) * 128],
                                                                     sp_ym[ec][:, tc * 128:(tc + 1) * 128]),
                          waits=[t_ym_all] + ([pr.tok("s_pe"), pr.tok("s_dv")] if ec == 0 else []),
                          inc=("s_rl", 16))
            psn = "psA" if tc % 2 == 0 else "psB"
            ps = PS[psn]
            t_mm = E_("tensor", lambda t, ps=ps, t_yl=t_yl: [t.matmul(ps[:, h * 512:(h + 1) * 512],
                                                                     ymst[:, ec * 128:(ec + 1) * 128],
                                                                     wB4v(ec * 1024 + h * 512, 512),
                                                                     start=(ec == 0), stop=(ec == NEC - 1))
                                                             for h in range(2) for ec in range(NEC)][-1],
                      waits=[t_yl, t_w_outp] + ps_free[psn], inc=("s_pe", 1))
            if "attn" in parts:
                t_dep = E_("vector", lambda v, tc=tc, ps=ps: v.tensor_copy(yacc[:, tc * DM:(tc + 1) * DM], ps[:]),
                           waits=[t_mm, pr.tok("s_out")], inc=("s_dv", 1))
            else:
                t_dep = E_("vector", lambda v, tc=tc: v.memset(yacc[:, tc * DM:(tc + 1) * DM], 0.0),
                           waits=[t_mm, pr.tok("s_out")], inc=("s_dv", 1))
            ps_free[psn] = [t_dep]
        t_phE_pe = pr.tok("s_pe")

        if debug and first:
            tdv = pr.tok("s_dv")
            E_("sync", lambda s: s.dma_start(d_tap_att, yacc[:]), waits=[tdv], inc=("s_tap", 16))

        if stop_i < PHASES.index("F"):
            continue
        if "moe" in parts:
            for e in range(NE):
                wf_wait = [t_phE_pe, pr.tok("s_dv"), pr.tok("s_tap")] if e == 0 else [pr.tok("s_pe")]
                if first:
                    wf_wait = wf_wait + [ct[("fc1", e)], ct[("fc2", e)]]
                tw1 = None
                for dmc in range(NDMC):
                    tw1 = E_("sync", lambda s, dmc=dmc, e=e: s.dma_start(w8v(dmc * 2048, 2 * FF),
                                                                        sc_fc1T[e][dmc * 128:(dmc + 1) * 128, :]),
                             waits=wf_wait, inc=("s_w", 16))
                tw2 = None
                for fc in range(8):
                    tw2 = E_("sync", lambda s, fc=fc, e=e: s.dma_start(w8v(16384 + fc * 1024, DM),
                                                                      sc_fc2T[e][fc * 128:(fc + 1) * 128, :]),
                             waits=wf_wait, inc=("s_w", 16))
                t_sc2 = None
                for fc in range(8):
                    t_sc2 = E_("vector", lambda v, fc=fc, e=e: v.tensor_scalar_mul(w8v(16384 + fc * 1024, DM),
                                                                                   w8v(16384 + fc * 1024, DM),
                                                                                   gmult[:, e * 8 + fc:e * 8 + fc + 1]),
                               waits=[tw2], inc=("s_dv", 1))
                for j in range(8):
                    psgn, psxn = ("psA", "psB") if j % 2 == 0 else ("psB", "psA")
                    psg_, psx_ = PS[psgn], PS[psxn]
                    t_mmg = E_("tensor", lambda t, j=j, psg_=psg_: [t.matmul(psg_[:, h * 512:(h + 1) * 512],
                                                                             w8v(dmc * 2048 + (2 * j + 1) * 128, 128),
                                                                             xnT[:, dmc * TS + h * 512: dmc * TS + (h + 1) * 512],
                                                                             start=(dmc == 0), stop=(dmc == NDMC - 1))
                                                                    for h in range(2) for dmc in range(NDMC)][-1],
                               waits=[tw2] + ps_free[psgn], inc=("s_pe", 1))
                    t_gl = E_("scalar", lambda sc, psg_=psg_, e=e, j=j:
                              sc.activation(stg[:, 0:TS], psg_[:], AF.Gelu,
                                            bias=fc1bs[:, e * 16 + 8 + j:e * 16 + 9 + j]),
                              waits=[t_mmg, pr.tok("s_dv"), pr.tok("s_rl")], inc=("s_ac", 1))
                    ps_free[psgn] = [t_gl]
                    t_mmx = E_("tensor", lambda t, j=j, psx_=psx_: [t.matmul(psx_[:, h * 512:(h + 1) * 512],
                                                                             w8v(dmc * 2048 + 2 * j * 128, 128),
                                                                             xnT[:, dmc * TS + h * 512: dmc * TS + (h + 1) * 512],
                                                                             start=(dmc == 0), stop=(dmc == NDMC - 1))
                                                                    for h in range(2) for dmc in range(NDMC)][-1],
                               waits=[tw2] + ps_free[psxn], inc=("s_pe", 1))
                    t_ap = E_("vector", lambda v, psx_=psx_, e=e, j=j:
                              v.scalar_tensor_tensor(apbv(j), psx_[:],
                                                     fc1bs[:, e * 16 + j:e * 16 + j + 1], stg[:, 0:TS],
                                                     op0=ALU.add, op1=ALU.mult),
                              waits=[t_mmx, t_gl], inc=("s_dv", 1))
                    ps_free[psxn] = [t_ap]
                t_ap_done = pr.tok("s_dv")
                for tc in range(NTC):
                    t_mm2 = E_("tensor", lambda t, tc=tc: [t.matmul(psC[:, h * 512:(h + 1) * 512],
                                                                    apbv(j)[:, tc * 128:(tc + 1) * 128],
                                                                    w8v(16384 + j * 1024 + h * 512, 512),
                                                                    start=(j == 0), stop=(j == 7))
                                                           for h in range(2) for j in range(8)][-1],
                               waits=[t_sc2, t_ap_done] + ps_free["psC"], inc=("s_pe", 1))
                    t_dep = E_("vector", lambda v, tc=tc, e=e:
                               v.scalar_tensor_tensor(yacc[:, tc * DM:(tc + 1) * DM], psC[:],
                                                      comb[:, tc * NE + e:tc * NE + e + 1],
                                                      yacc[:, tc * DM:(tc + 1) * DM],
                                                      op0=ALU.mult, op1=ALU.add),
                               waits=[t_mm2, t_comb_done], inc=("s_dv", 1))
                    ps_free["psC"] = [t_dep]
        t_phF_pe = pr.tok("s_pe")
        t_phF_dv = pr.tok("s_dv")

        if stop_i < PHASES.index("G"):
            continue
        if "shared" in parts:
            tw_s2 = None
            for fsc in range(NEC):
                tw_s2 = E_("sync", lambda s, fsc=fsc: s.dma_start(wB4v(fsc * 1024, DM),
                                                                  sc_sfc2T[fsc * 128:(fsc + 1) * 128, :]),
                           waits=[t_phF_pe] + ([ct["sfc2"]] if first else []), inc=("s_w2", 16))
            t_ssc = None
            for fsc in range(NEC):
                t_ssc = E_("vector", lambda v, fsc=fsc: v.tensor_scalar_mul(wB4v(fsc * 1024, DM),
                                                                            wB4v(fsc * 1024, DM),
                                                                            smult[:, fsc:fsc + 1]),
                           waits=[tw_s2, t_phF_dv], inc=("s_dv", 1))
            slot_tok = [None, None]
            slot_sem_g = ["s_wa", "s_wb"]
            for j in range(NEC):
                sl = j % 2
                base = 16384 + sl * 2048
                t_wj = None
                for dmc in range(NDMC):
                    w0 = ([t_phF_pe if j < 2 else pr.tok("s_pe")] +
                          ([slot_tok[sl]] if slot_tok[sl] else []) +
                          ([pr.tok("s_dv")] if j == 0 else []) +
                          ([ct["sfc1"]] if (first and j < 2) else [])) if dmc == 0 else []
                    t_wj = E_("sync", lambda s, j=j, base=base, dmc=dmc:
                              s.dma_start(w8[:, base + dmc * 256: base + dmc * 256 + 256],
                                          sc_sfc1T[dmc * 128:(dmc + 1) * 128, :]
                                          .rearrange("p (s a c) -> p a s c", s=2, c=128)[:, j]),
                              waits=w0, inc=(slot_sem_g[sl], 16))
                psgn, psxn = ("psA", "psB") if j % 2 == 0 else ("psB", "psA")
                psg_, psx_ = PS[psgn], PS[psxn]
                t_mmg = E_("tensor", lambda t, base=base, psg_=psg_: [t.matmul(psg_[:, h * 512:(h + 1) * 512],
                                                                               w8[:, base + dmc * 256 + 128: base + dmc * 256 + 256],
                                                                               xnT[:, dmc * TS + h * 512: dmc * TS + (h + 1) * 512],
                                                                               start=(dmc == 0), stop=(dmc == NDMC - 1))
                                                                      for h in range(2) for dmc in range(NDMC)][-1],
                           waits=[t_wj] + ps_free[psgn], inc=("s_pe", 1))
                t_gl = E_("scalar", lambda sc, psg_=psg_, j=j: sc.activation(stg[:, 0:TS], psg_[:], AF.Gelu,
                                                                             bias=sfc1bs[:, 16 + j:17 + j]),
                          waits=[t_mmg, pr.tok("s_dv")], inc=("s_ac", 1))
                ps_free[psgn] = [t_gl]
                t_mmx = E_("tensor", lambda t, base=base, psx_=psx_: [t.matmul(psx_[:, h * 512:(h + 1) * 512],
                                                                               w8[:, base + dmc * 256: base + dmc * 256 + 128],
                                                                               xnT[:, dmc * TS + h * 512: dmc * TS + (h + 1) * 512],
                                                                               start=(dmc == 0), stop=(dmc == NDMC - 1))
                                                                      for h in range(2) for dmc in range(NDMC)][-1],
                           waits=[t_wj] + ps_free[psxn], inc=("s_pe", 1))
                t_as = E_("vector", lambda v, psx_=psx_, j=j:
                          v.scalar_tensor_tensor(w8[:, j * 1024:(j + 1) * 1024],
                                                 psx_[:], sfc1bs[:, j:j + 1], stg[:, 0:TS],
                                                 op0=ALU.add, op1=ALU.mult),
                          waits=[t_mmx, t_gl], inc=("s_dv", 1))
                slot_tok[sl] = t_mmx
                ps_free[psxn] = [t_as]
            t_as_done = pr.tok("s_dv")
            for tc in range(NTC):
                E_("tensor", lambda t, tc=tc: [t.matmul(psC[:, h * 512:(h + 1) * 512],
                                                        w8[:, j * 1024 + tc * 128: j * 1024 + (tc + 1) * 128],
                                                        wB4v(j * 1024 + h * 512, 512),
                                                        start=(j == 0), stop=False)
                                               for h in range(2) for j in range(NEC)][-1],
                   waits=[t_as_done, t_ssc] + ps_free["psC"], inc=("s_pe", 1))
                if "moe" in parts:
                    E_("tensor", lambda t, tc=tc: [t.matmul(psC[:, h * 512:(h + 1) * 512],
                                                            combT[0:NE, tc * 128:(tc + 1) * 128],
                                                            fc2bias[0:NE, h * 512:(h + 1) * 512],
                                                            start=False, stop=False) for h in range(2)][-1],
                       waits=[t_xnT_done, t_gp_all], inc=("s_pe", 1))
                t_mmb2 = E_("tensor", lambda t, tc=tc: [t.matmul(psC[:, h * 512:(h + 1) * 512],
                                                                 onesb[0:1, 0:128],
                                                                 sfc2bias[0:1, h * 512:(h + 1) * 512],
                                                                 start=False, stop=True) for h in range(2)][-1],
                            waits=[t_gp_all], inc=("s_pe", 1))
                t_dep = E_("vector", lambda v, tc=tc: v.tensor_add(yacc[:, tc * DM:(tc + 1) * DM],
                                                                   yacc[:, tc * DM:(tc + 1) * DM], psC[:]),
                           waits=[t_mmb2], inc=("s_dv", 1))
                ps_free["psC"] = [t_dep]

        if stop_i < PHASES.index("H"):
            continue
        for tc in range(NTC):
            slot = tc % 2
            xv = stg[:, slot * TS: slot * TS + DM]
            if "resid" in parts:
                t_x = E_("sync", lambda s, b=b, tb=tb, tc=tc, xv=xv:
                         s.dma_start(xv, d_x[b, tb + tc * 128: tb + (tc + 1) * 128, :]),
                         waits=[pr.tok("s_dv"), pr.tok("s_ac")], inc=("s_x", 16))
                t_add = E_("vector", lambda v, tc=tc, xv=xv: v.tensor_add(yacc[:, tc * DM:(tc + 1) * DM],
                                                                          yacc[:, tc * DM:(tc + 1) * DM], xv),
                           waits=[t_x], inc=("s_dv", 1))
            else:
                t_add = pr.tok("s_dv")
            E_("sync", lambda s, b=b, tb=tb, tc=tc: s.dma_start(d_out[b, tb + tc * 128: tb + (tc + 1) * 128, :],
                                                                yacc[:, tc * DM:(tc + 1) * DM]),
               waits=[t_add], inc=("s_out", 16))

        prev_w8_free = pr.tok("s_pe")
        prev_wB4_free = pr.tok("s_pe")

    E_("vector", lambda v: None,
       waits=[pr.tok(s) for s in ("s_out", "s_in", "s_w", "s_wt", "s_wa", "s_wb",
                                  "s_w2", "s_sp", "s_rl", "s_bc", "s_gp", "s_ym", "s_x", "s_tap")])

    with stack:
        with nc.Block() as blk:
            OWN = {"vector": "s_dv", "scalar": "s_ac", "gpsimd": "s_gp"}

            def realize(engname):
                own = OWN.get(engname)

                def body(eng):
                    prev = 0
                    for waits, fn, incv in pr.ops[engname]:
                        seen = False
                        for (s, v) in waits:
                            if own is not None and s == own:
                                v = max(v, prev)
                                seen = True
                            eng.wait_ge(sems[s], v)
                        if own is not None and not seen and prev > 0:
                            eng.wait_ge(sems[own], prev)
                        inst = fn(eng) if fn is not None else None
                        if isinstance(inst, list):
                            inst = inst[-1]
                        if incv is not None:
                            sname, delta = incv
                            assert inst is not None, f"inc without instruction on {engname}"
                            inst.then_inc(sems[sname], delta)
                            if sname == own:
                                prev += delta
                return body

            blk.sync(realize("sync"))
            blk.vector(realize("vector"))
            blk.scalar(realize("scalar"))
            blk.tensor(realize("tensor"))
            blk.gpsimd(realize("gpsimd"))
    return nc


def make_in_maps(inputs):
    import ml_dtypes
    f32 = np.float32
    x = np.ascontiguousarray(inputs["L0_attn_x"], dtype=f32)
    shards = np.split(x, 8, axis=0)

    def A(v):
        return np.ascontiguousarray(v, dtype=f32)

    def colmajor(v, k):
        return A(np.asarray(v, dtype=f32).reshape(k, 128).T)

    inproj = np.asarray(inputs["L0_attn_in_proj_w"], dtype=f32)
    dtw = np.asarray(inputs["L0_attn_dt_w"], dtype=f32)
    outp = np.asarray(inputs["L0_attn_out_proj_w"], dtype=f32)
    sfc1 = np.asarray(inputs["L0_ffn_s_fc1_w"], dtype=f32)
    sfc2 = np.asarray(inputs["L0_ffn_s_fc2_w"], dtype=f32)
    fc1 = np.asarray(inputs["L0_ffn_fc1_w"], dtype=f32)
    fc2 = np.asarray(inputs["L0_ffn_fc2_w"], dtype=f32)
    gate = np.asarray(inputs["L0_ffn_gate_w"], dtype=f32)
    wB = np.asarray(inputs["L0_attn_B_proj_w"], dtype=f32)
    wC = np.asarray(inputs["L0_attn_C_proj_w"], dtype=f32)
    Alog = np.asarray(inputs["L0_attn_A_log"], dtype=f32)
    convw = np.asarray(inputs["L0_attn_conv_w"], dtype=f32)
    gmult = np.asarray(inputs["L0_ffn_geglu_mult"], dtype=f32)
    fc1b = np.asarray(inputs["L0_ffn_fc1_b"], dtype=f32)

    fc1T = fc1.transpose(0, 2, 1)
    fc1P = fc1T.reshape(NE, DM, 2, 8, 128).transpose(0, 1, 3, 2, 4).reshape(NE, DM, 2 * FF)

    base = {
        "w_inprojT": A(inproj.T),
        "w_dtT": A(dtw.T),
        "w_outpT": A(outp.T),
        "w_sfc1T": A(sfc1.T),
        "w_sfc2T": A(sfc2.T),
        "w_fc1P": A(fc1P),
        "w_fc2T": A(fc2.transpose(0, 2, 1)),
        "w_gateT": A(gate.T.reshape(NDMC, 128, NE).transpose(1, 0, 2).reshape(128, NDMC * NE)),
        "w_bcT": A(np.concatenate(
            [wB.T.reshape(NEC, 128, NST).transpose(1, 0, 2),
             wC.T.reshape(NEC, 128, NST).transpose(1, 0, 2)], axis=2).reshape(128, NEC * 2 * NST)),
        "A_log2": A(Alog.reshape(NEC, 128, NST).transpose(1, 0, 2).reshape(128, NEC * NST)),
        "w_conv2": A(convw[:, 0, :].reshape(NEC, 128, 4).transpose(1, 0, 2).reshape(128, NEC * 4)),
        "b_conv2": colmajor(inputs["L0_attn_conv_b"], NEC),
        "b_dt2": colmajor(inputs["L0_attn_dt_b"], NEC),
        "D_par2": colmajor(inputs["L0_attn_D_param"], NEC),
        "b_sfc1_2": colmajor(inputs["L0_ffn_s_fc1_b"], 32),
        "s_mult2": colmajor(inputs["L0_ffn_s_geglu_mult"], NST),
        "g_mult2": A(gmult.reshape(NE, 8, 128).transpose(2, 0, 1).reshape(128, NE * 8)),
        "b_fc1_2": A(fc1b.reshape(NE, 16, 128).transpose(2, 0, 1).reshape(128, NE * 16)),
        "b_fc2": A(inputs["L0_ffn_fc2_b"]),
        "b_sfc2": A(inputs["L0_ffn_s_fc2_b"]),
    }
    aux = {
        "ident_bf": np.eye(128, dtype=ml_dtypes.bfloat16),
        "ones_bf": np.ones((1, 128), dtype=ml_dtypes.bfloat16),
    }
    return [dict(base, x=s, **aux) for s in shards]


def kernel(**inputs):
    nc = build_core_program()
    in_maps = make_in_maps(inputs)
    res = run_bass_kernel_spmd(nc, in_maps, core_ids=list(range(8)))
    return np.concatenate([r["out"] for r in res.results], axis=0)


# version 07
# speedup vs baseline: 1.1574x; 1.1040x over previous
import numpy as np
import concourse.bass as bass
import concourse.mybir as mybir
from concourse.bass_utils import run_bass_kernel_spmd

F32 = mybir.dt.float32
BF16 = mybir.dt.bfloat16
AF = mybir.ActivationFunctionType
ALU = mybir.AluOpType
AX = mybir.AxisListType

DM = 1024
E2 = 4096
E = 2048
NST = 16
NE = 15
FF = 1024
FS = 2048
BLOC = 4
L = 2048
TS = 1024
NSB = BLOC * L // TS
NTC = TS // 128
NEC = E // 128
NDMC = DM // 128
ECG = 1
NG = NEC // ECG
EPS = 1e-5

ENGINES = ("sync", "vector", "scalar", "tensor", "gpsimd")


class Prog:
    def __init__(self):
        self.ops = {e: [] for e in ENGINES}
        self.cnt = {}

    def tok(self, sem):
        return (sem, self.cnt.get(sem, 0))

    def emit(self, eng, fn=None, waits=(), inc=None):
        tok = None
        incv = None
        if inc is not None:
            sem, d = inc
            self.cnt[sem] = self.cnt.get(sem, 0) + d
            tok = (sem, self.cnt[sem])
            incv = (sem, d)
        ws = tuple(w for w in waits if w is not None and w[1] > 0)
        self.ops[eng].append((ws, fn, incv))
        return tok


PHASES = ("p0", "A", "B", "Ca", "C", "D", "E", "F", "G", "H")


def build_core_program(parts=("attn", "moe", "shared", "resid"), debug=False,
                       nsb=NSB, stop="H", ca_mode=3):
    stop_i = PHASES.index(stop)
    nc = bass.Bass("TRN2", target_bir_lowering=False)
    pr = Prog()

    def din(name, shape):
        return nc.dram_tensor(name, list(shape), F32, kind="ExternalInput").ap()

    d_x = din("x", (BLOC, L, DM))
    d_inprojT = din("w_inprojT", (DM, E2))
    d_dtwT = din("w_dtT", (E, E))
    d_outpT = din("w_outpT", (E, DM))
    d_sfc1T = din("w_sfc1T", (DM, 2 * FS))
    d_sfc2T = din("w_sfc2T", (FS, DM))
    d_fc1P = din("w_fc1P", (NE, DM, 2 * FF))
    d_fc2T = din("w_fc2T", (NE, FF, DM))
    d_gateT = din("w_gateT", (128, NDMC * NE))
    d_bcT = din("w_bcT", (128, NEC * 2 * NST))
    d_Alog2 = din("A_log2", (128, NEC * NST))
    d_convw2 = din("w_conv2", (128, NEC * 4))
    d_convb2 = din("b_conv2", (128, NEC))
    d_dtb2 = din("b_dt2", (128, NEC))
    d_D2 = din("D_par2", (128, NEC))
    d_sfc1b2 = din("b_sfc1_2", (128, 32))
    d_smult2 = din("s_mult2", (128, NST))
    d_gmult2 = din("g_mult2", (128, NE * 8))
    d_fc1b2 = din("b_fc1_2", (128, NE * 16))
    d_fc2b = din("b_fc2", (NE, DM))
    d_sfc2b = din("b_sfc2", (DM,))
    d_ident = nc.dram_tensor("ident_bf", [128, 128], BF16, kind="ExternalInput").ap()
    d_ones = nc.dram_tensor("ones_bf", [1, 128], BF16, kind="ExternalInput").ap()

    d_out = nc.dram_tensor("out", [BLOC, L, DM], F32, kind="ExternalOutput").ap()

    def dint(name, shape, dt):
        return nc.dram_tensor(name, list(shape), dt, kind="Internal").ap()

    sc_inprojT = dint("sc_inprojT", (DM, E2), BF16)
    sc_dtwT = dint("sc_dtwT", (E, E), BF16)
    sc_outpT = dint("sc_outpT", (E, DM), BF16)
    sc_fc1T = dint("sc_fc1T", (NE, DM, 2 * FF), BF16)
    sc_fc2T = dint("sc_fc2T", (NE, FF, DM), BF16)
    sc_sfc1T = dint("sc_sfc1T", (DM, 2 * FS), BF16)
    sc_sfc2T = dint("sc_sfc2T", (FS, DM), BF16)
    sp_dt = dint("sp_dt", (NEC, 128, TS), F32)
    sp_xs = dint("sp_xs", (NEC, 128, TS), F32)
    sp_z = dint("sp_z", (NEC, 128, TS), F32)
    sp_ym = dint("sp_ym", (NEC, 128, TS), BF16)
    sc_bc = dint("sc_bc", (2 * NST, TS), F32)

    kind_dbg = "ExternalOutput" if debug else "Internal"
    d_tap_comb = nc.dram_tensor("tap_comb", [128, NTC * NE], F32, kind=kind_dbg).ap()
    d_tap_bc = nc.dram_tensor("tap_bc", [2 * NST, TS], F32, kind=kind_dbg).ap()
    d_tap_att = nc.dram_tensor("tap_att", [128, NTC * DM], F32, kind=kind_dbg).ap()
    d_tap_xnT = nc.dram_tensor("tap_xnT", [128, NDMC * TS], BF16, kind=kind_dbg).ap()
    d_tap_lnst = nc.dram_tensor("tap_lnst", [128, 32], F32, kind=kind_dbg).ap()

    import contextlib
    stack = contextlib.ExitStack()

    def sbuf(name, shape, dt=F32):
        return stack.enter_context(nc.sbuf_tensor(name, list(shape), dt))

    w8 = sbuf("w8", (128, 24576), BF16)
    wB4 = sbuf("wB4", (128, 16384), BF16)
    xnT = sbuf("xnT", (128, NDMC * TS), BF16)
    yacc = sbuf("yacc", (128, NTC * DM), F32)
    trio = sbuf("trio", (128, 8192), BF16)
    ymst = sbuf("ymst", (128, 2 * TS), BF16)
    stg = sbuf("stg", (128, 2 * TS), F32)
    rlbuf = sbuf("rlbuf", (128, 2 * TS), F32)
    bcrep = sbuf("bcrep", (128, 4 * TS), F32)
    tmpa = sbuf("tmpa", (128, TS), F32)
    tmpb = sbuf("tmpb", (128, TS), F32)
    tmph = sbuf("tmph", (128, TS), F32)
    combT = sbuf("combT", (NST, TS), BF16)
    idb = sbuf("idb", (128, 128), BF16)
    gateT = sbuf("gateT", (128, NDMC * NE), BF16)
    bcT = sbuf("bcT", (128, NEC * 2 * NST), BF16)
    fc2bias = sbuf("fc2bias", (NE, DM), BF16)
    sfc2bias = sbuf("sfc2bias", (1, DM), BF16)
    onesb = sbuf("onesb", (1, 128), BF16)
    combb = sbuf("combb", (128, NE), BF16)
    xpadv = rlbuf[:, 0:TS + 3]
    sqtv = rlbuf[:, TS:2 * TS]
    tails = sbuf("tails", (128, NEC * 3), F32)
    hstate = sbuf("hstate", (128, NEC * NST), F32)
    A_sb = sbuf("A_sb", (128, NEC * NST), F32)
    convw = sbuf("convw", (128, NEC * 4), F32)
    convb = sbuf("convb", (128, NEC), F32)
    dtb = sbuf("dtb", (128, NEC), F32)
    D_sb = sbuf("D_sb", (128, NEC), F32)
    gmult = sbuf("gmult", (128, NE * 8), F32)
    fc1bs = sbuf("fc1bs", (128, NE * 16), F32)
    sfc1bs = sbuf("sfc1bs", (128, 32), F32)
    smult = sbuf("smult", (128, NST), F32)
    comb = sbuf("comb", (128, NTC * NE), F32)
    lnst = sbuf("lnst", (128, 32), F32)

    trioF = trio[:].bitcast(F32)

    def dtg(i):
        return trioF[:, i * TS:(i + 1) * TS]

    def ug(i):
        return trioF[:, (ECG + i) * TS:(ECG + i + 1) * TS]

    def ysg(i):
        return trioF[:, (2 * ECG + i) * TS:(2 * ECG + i + 1) * TS]

    def apbv(j):
        return trio[:, j * TS:(j + 1) * TS]

    psA = stack.enter_context(nc.psum_tensor("psA", [128, 1024], F32))
    psB = stack.enter_context(nc.psum_tensor("psB", [128, 1024], F32))
    psC = stack.enter_context(nc.psum_tensor("psC", [128, 1024], F32))
    psT = stack.enter_context(nc.psum_tensor("psT", [128, 1024], BF16))
    psG = stack.enter_context(nc.psum_tensor("psG", [128, 512], F32))
    PS = {"psA": psA, "psB": psB, "psC": psC}

    sems = {}
    for s in ["s_in", "s_w", "s_sp", "s_rl", "s_bc", "s_b0", "s_b1", "s_out", "s_pe", "s_dv",
              "s_ac", "s_gp", "s_ym", "s_x", "s_wt", "s_wa", "s_wb", "s_w2", "s_tap"]:
        sems[s] = stack.enter_context(nc.semaphore(s))

    E_ = pr.emit
    ps_free = {"psA": [], "psB": [], "psC": [], "psT": [], "psG": []}

    ct = {}
    ct["inproj"] = E_("gpsimd", lambda g: g.dma_start(sc_inprojT, d_inprojT), inc=("s_gp", 16))
    ct["dtw"] = E_("gpsimd", lambda g: g.dma_start(sc_dtwT, d_dtwT), inc=("s_gp", 16))
    for e in range(NE):
        ct[("fc1", e)] = E_("gpsimd", lambda g, e=e: g.dma_start(sc_fc1T[e], d_fc1P[e]), inc=("s_gp", 16))
        ct[("fc2", e)] = E_("gpsimd", lambda g, e=e: g.dma_start(sc_fc2T[e], d_fc2T[e]), inc=("s_gp", 16))
    ct["outp"] = E_("gpsimd", lambda g: g.dma_start(sc_outpT, d_outpT), inc=("s_gp", 16))
    ct["sfc1"] = E_("gpsimd", lambda g: g.dma_start(sc_sfc1T, d_sfc1T), inc=("s_gp", 16))
    ct["sfc2"] = E_("gpsimd", lambda g: g.dma_start(sc_sfc2T, d_sfc2T), inc=("s_gp", 16))
    E_("gpsimd", lambda g: g.dma_start(fc2bias[:], d_fc2b), inc=("s_gp", 16))
    E_("gpsimd", lambda g: g.dma_start(sfc2bias[:], d_sfc2b.rearrange("(o d) -> o d", o=1)), inc=("s_gp", 16))
    E_("gpsimd", lambda g: g.dma_start(gateT[:], d_gateT), inc=("s_gp", 16))
    E_("gpsimd", lambda g: g.dma_start(bcT[:], d_bcT), inc=("s_gp", 16))
    t_gp_all = pr.tok("s_gp")

    E_("sync", lambda s: s.dma_start(idb[:], d_ident), inc=("s_in", 16))
    E_("sync", lambda s: s.dma_start(onesb[:], d_ones), inc=("s_in", 16))
    E_("sync", lambda s: s.dma_start(A_sb[:], d_Alog2), inc=("s_in", 16))
    E_("sync", lambda s: s.dma_start(convw[:], d_convw2), inc=("s_in", 16))
    E_("sync", lambda s: s.dma_start(convb[:], d_convb2), inc=("s_in", 16))
    E_("sync", lambda s: s.dma_start(dtb[:], d_dtb2), inc=("s_in", 16))
    E_("sync", lambda s: s.dma_start(D_sb[:], d_D2), inc=("s_in", 16))
    E_("sync", lambda s: s.dma_start(sfc1bs[:], d_sfc1b2), inc=("s_in", 16))
    E_("sync", lambda s: s.dma_start(smult[:], d_smult2), inc=("s_in", 16))
    E_("sync", lambda s: s.dma_start(gmult[:], d_gmult2), inc=("s_in", 16))
    t_A_ld = E_("sync", lambda s: s.dma_start(fc1bs[:], d_fc1b2), inc=("s_in", 16))

    t_Ae = E_("scalar", lambda sc: sc.activation(A_sb[:], A_sb[:], AF.Exp),
              waits=[pr.tok("s_in")], inc=("s_ac", 1))
    E_("vector", lambda v: v.tensor_scalar_mul(A_sb[:], A_sb[:], -1.0),
       waits=[t_Ae], inc=("s_dv", 1))

    t_p0_ac = pr.tok("s_ac")
    t_p0_in = pr.tok("s_in")
    E_("vector", lambda v: v.memset(hstate[:], 0.0), inc=("s_dv", 1))
    E_("vector", lambda v: v.memset(tails[:], 0.0), inc=("s_dv", 1))
    t_p0_dv = pr.tok("s_dv")

    def w8v(c0, w):
        return w8[:, c0:c0 + w]

    def wB4v(c0, w):
        return wB4[:, c0:c0 + w]

    xsbv = wB4

    prev_w8_free = None
    prev_wB4_free = None

    for sb_i in range(nsb):
        b = sb_i // 2
        half = sb_i % 2
        tb = half * TS
        first = sb_i == 0

        w_wait = [ct["inproj"]] if first else [prev_w8_free]
        t_w_inproj = None
        for dmc in range(NDMC):
            t_w_inproj = E_("sync", lambda s, dmc=dmc: s.dma_start(w8v(dmc * 2048, 2048),
                                                                   sc_inprojT[dmc * 128:(dmc + 1) * 128, 0:2048]),
                            waits=w_wait, inc=("s_w", 16))

        if stop_i < PHASES.index("A"):
            continue
        waitsA = [t_p0_dv, t_p0_in, t_gp_all, t_p0_ac] if first else []
        for tc in range(NTC):
            xv = stg[:, 0:DM]
            t_x = E_("sync", lambda s, b=b, tb=tb, tc=tc, xv=xv:
                     s.dma_start(xv, d_x[b, tb + tc * 128: tb + (tc + 1) * 128, :]),
                     waits=waitsA + [pr.tok("s_dv"), pr.tok("s_ac"), pr.tok("s_tap")], inc=("s_x", 16))
            c0 = (tc % 2) * 8
            t1 = E_("vector", lambda v, xv=xv, c0=c0: v.reduce_sum(lnst[:, c0:c0 + 1], xv, axis=AX.X),
                    waits=[t_x], inc=("s_dv", 1))
            t2 = E_("vector", lambda v, c0=c0: v.tensor_scalar_mul(lnst[:, c0 + 1:c0 + 2], lnst[:, c0:c0 + 1], 1.0 / DM),
                    waits=[t1], inc=("s_dv", 1))
            t3 = E_("vector", lambda v, xv=xv, c0=c0: v.tensor_single_scalar(sqtv, xv, lnst[:, c0 + 1:c0 + 2],
                                                                             op=ALU.subtract),
                    waits=[t2, pr.tok("s_ac"), pr.tok("s_rl")], inc=("s_dv", 1))
            t4 = E_("scalar", lambda sc: sc.activation(sqtv, sqtv, AF.Square), waits=[t3], inc=("s_ac", 1))
            t5 = E_("vector", lambda v, c0=c0: v.reduce_sum(lnst[:, c0 + 2:c0 + 3], sqtv, axis=AX.X),
                    waits=[t4], inc=("s_dv", 1))
            t5b = E_("vector", lambda v, c0=c0: v.tensor_scalar(lnst[:, c0 + 2:c0 + 3], lnst[:, c0 + 2:c0 + 3],
                                                                1.0 / DM, EPS, op0=ALU.mult, op1=ALU.add),
                     waits=[t5], inc=("s_dv", 1))
            t6 = E_("scalar", lambda sc, c0=c0: sc.activation(lnst[:, c0 + 3:c0 + 4], lnst[:, c0 + 2:c0 + 3],
                                                              AF.Ln),
                    waits=[t5b], inc=("s_ac", 1))
            t7 = E_("vector", lambda v, c0=c0: v.tensor_scalar_mul(lnst[:, c0 + 4:c0 + 5], lnst[:, c0 + 3:c0 + 4], -0.5),
                    waits=[t6], inc=("s_dv", 1))
            t8 = E_("scalar", lambda sc, c0=c0: sc.activation(lnst[:, c0 + 5:c0 + 6], lnst[:, c0 + 4:c0 + 5], AF.Exp),
                    waits=[t7], inc=("s_ac", 1))
            t9 = E_("vector", lambda v, xv=xv, c0=c0:
                    v.tensor_scalar(sqtv, xv, lnst[:, c0 + 1:c0 + 2], lnst[:, c0 + 5:c0 + 6],
                                    op0=ALU.subtract, op1=ALU.mult),
                    waits=[t8], inc=("s_dv", 1))
            t10 = E_("vector", lambda v: v.tensor_copy(ymst[:, 0:DM], sqtv),
                     waits=[pr.tok("s_pe")], inc=("s_dv", 1))
            t_tr = E_("tensor", lambda t: [t.transpose(psT[:, dmc * 128:(dmc + 1) * 128],
                                                       ymst[:, dmc * 128:(dmc + 1) * 128], idb[:])
                                           for dmc in range(NDMC)][-1],
                      waits=[t10] + ps_free["psT"], inc=("s_pe", 1))
            t_cp = E_("scalar", lambda sc, tc=tc:
                      sc.copy(xnT.rearrange("p (a t) -> p a t", a=NDMC)[:, :, tc * 128:(tc + 1) * 128],
                              psT.rearrange("p (a t) -> p a t", a=NDMC)[:, :, 0:128]),
                      waits=[t_tr], inc=("s_ac", 1))
            ps_free["psT"] = [t_cp]

            t_g = E_("tensor", lambda t, tc=tc: [t.matmul(psG[:, 0:NE],
                                                          xnT[:, dmc * TS + tc * 128: dmc * TS + (tc + 1) * 128],
                                                          gateT[:, dmc * NE:(dmc + 1) * NE],
                                                          start=(dmc == 0), stop=(dmc == NDMC - 1))
                                                 for dmc in range(NDMC)][-1],
                     waits=[t_cp] + ps_free["psG"], inc=("s_pe", 1))
            cc = tc * NE
            t_sm1 = E_("vector", lambda v: v.reduce_max(lnst[:, 16:17], psG[:, 0:NE], axis=AX.X),
                       waits=[t_g, pr.tok("s_ac")], inc=("s_dv", 1))
            t_negm = E_("vector", lambda v: v.tensor_scalar_mul(lnst[:, 17:18], lnst[:, 16:17], -1.0),
                        waits=[t_sm1], inc=("s_dv", 1))
            t_sm2 = E_("scalar", lambda sc, cc=cc: sc.activation(comb[:, cc:cc + NE], psG[:, 0:NE], AF.Exp,
                                                                 bias=lnst[:, 17:18]),
                       waits=[t_negm], inc=("s_ac", 1))
            t_sm3 = E_("vector", lambda v, cc=cc: v.reduce_sum(lnst[:, 18:19], comb[:, cc:cc + NE], axis=AX.X),
                       waits=[t_sm2], inc=("s_dv", 1))
            t_rc = E_("vector", lambda v: v.reciprocal(lnst[:, 19:20], lnst[:, 18:19]),
                      waits=[t_sm3], inc=("s_dv", 1))
            t_scl = E_("vector", lambda v, cc=cc: v.tensor_single_scalar(comb[:, cc:cc + NE], comb[:, cc:cc + NE],
                                                                         lnst[:, 19:20], op=ALU.mult),
                       waits=[t_rc], inc=("s_dv", 1))
            ps_free["psG"] = [t_sm3, t_sm2]
            t_k0 = E_("vector", lambda v, cc=cc: v.tensor_copy(sqtv[:, 0:NE], comb[:, cc:cc + NE]),
                      waits=[t_scl], inc=("s_dv", 1))
            t_k1 = E_("vector", lambda v: v.memset(sqtv[:, 16:16 + NE], 0.0), inc=("s_dv", 1))
            t_k2 = E_("vector", lambda v: v.memset(lnst[:, 20:21], 0.0), inc=("s_dv", 1))
            t_prev = t_k2
            for r in range(3):
                t_rm = E_("vector", lambda v: v.reduce_max(lnst[:, 21:22], sqtv[:, 0:NE], axis=AX.X),
                          waits=[t_prev], inc=("s_dv", 1))
                t_eq = E_("vector", lambda v: v.tensor_single_scalar(sqtv[:, 32:32 + NE], sqtv[:, 0:NE],
                                                                     lnst[:, 21:22], op=ALU.is_equal),
                          waits=[t_rm], inc=("s_dv", 1))
                E_("vector", lambda v: v.scalar_tensor_tensor(sqtv[:, 16:16 + NE], sqtv[:, 32:32 + NE],
                                                              lnst[:, 21:22], sqtv[:, 16:16 + NE],
                                                              op0=ALU.mult, op1=ALU.add),
                   waits=[t_eq], inc=("s_dv", 1))
                E_("vector", lambda v: v.tensor_add(lnst[:, 20:21], lnst[:, 20:21], lnst[:, 21:22]),
                   waits=[t_rm], inc=("s_dv", 1))
                t_prev = E_("vector", lambda v: v.scalar_tensor_tensor(sqtv[:, 0:NE], sqtv[:, 32:32 + NE], -1e30,
                                                                       sqtv[:, 0:NE], op0=ALU.mult, op1=ALU.add),
                            waits=[t_eq], inc=("s_dv", 1))
            t_dn = E_("vector", lambda v: v.tensor_scalar_add(lnst[:, 20:21], lnst[:, 20:21], 1e-20),
                      waits=[t_prev], inc=("s_dv", 1))
            t_rc2 = E_("vector", lambda v: v.reciprocal(lnst[:, 22:23], lnst[:, 20:21]),
                       waits=[t_dn], inc=("s_dv", 1))
            t_fin = E_("vector", lambda v, cc=cc: v.tensor_single_scalar(comb[:, cc:cc + NE], sqtv[:, 16:16 + NE],
                                                                         lnst[:, 22:23], op=ALU.mult),
                       waits=[t_rc2], inc=("s_dv", 1))
            t_cb = E_("vector", lambda v, cc=cc: v.tensor_copy(combb[:], comb[:, cc:cc + NE]),
                      waits=[t_fin, pr.tok("s_pe")], inc=("s_dv", 1))
            t_ctr = E_("tensor", lambda t: t.transpose(psT[0:NE, 0:128], combb[:], idb[:]),
                       waits=[t_cb] + ps_free["psT"], inc=("s_pe", 1))
            t_ctc = E_("scalar", lambda sc, tc=tc: sc.copy(combT[0:NE, tc * 128:(tc + 1) * 128], psT[0:NE, 0:128]),
                       waits=[t_ctr], inc=("s_ac", 1))
            ps_free["psT"] = [t_ctc]
        t_comb_done = pr.tok("s_dv")
        t_xnT_done = pr.tok("s_ac")

        if debug and first:
            E_("sync", lambda s: s.dma_start(d_tap_comb, comb[:]), waits=[t_comb_done], inc=("s_tap", 16))
            E_("sync", lambda s: s.dma_start(d_tap_xnT, xnT[:]), waits=[t_xnT_done], inc=("s_tap", 16))

        if stop_i < PHASES.index("B"):
            continue
        for m in range(2 * NEC):
            if m == NEC:
                t_half_pe = pr.tok("s_pe")
                for dmc in range(NDMC):
                    t_w_inproj = E_("sync", lambda s, dmc=dmc:
                                    s.dma_start(w8v(dmc * 2048, 2048),
                                                sc_inprojT[dmc * 128:(dmc + 1) * 128, 2048:4096]),
                                    waits=[t_half_pe], inc=("s_w", 16))
            psn = "psA" if m % 2 == 0 else "psB"
            ps = PS[psn]
            ml = m % NEC
            t_mm = E_("tensor", lambda t, ml=ml, ps=ps: [t.matmul(ps[:, h * 512:(h + 1) * 512],
                                                                  w8v(dmc * 2048 + ml * 128, 128),
                                                                  xnT[:, dmc * TS + h * 512: dmc * TS + (h + 1) * 512],
                                                                  start=(dmc == 0), stop=(dmc == NDMC - 1))
                                                         for h in range(2) for dmc in range(NDMC)][-1],
                      waits=[t_w_inproj, t_xnT_done] + ps_free[psn], inc=("s_pe", 1))
            slot = m % 2
            sv = stg[:, slot * TS:(slot + 1) * TS]
            if m < NEC:
                ec = m
                if half == 0:
                    t_ti = E_("vector", lambda v: v.memset(xpadv[:, 0:3], 0.0),
                              waits=[pr.tok("s_ac"), pr.tok("s_rl")], inc=("s_dv", 1))
                else:
                    t_ti = E_("vector", lambda v, ec=ec: v.tensor_copy(xpadv[:, 0:3], tails[:, ec * 3:(ec + 1) * 3]),
                              waits=[pr.tok("s_ac"), pr.tok("s_rl")], inc=("s_dv", 1))
                t_ev = E_("scalar", lambda sc, ps=ps: sc.copy(xpadv[:, 3:3 + TS], ps[:]),
                          waits=[t_mm, t_ti, pr.tok("s_dv")], inc=("s_ac", 1))
                ps_free[psn] = [t_ev]
                E_("vector", lambda v, ec=ec: v.tensor_copy(tails[:, ec * 3:(ec + 1) * 3], xpadv[:, TS:TS + 3]),
                   waits=[t_ev], inc=("s_dv", 1))
                t_c = E_("vector", lambda v, ec=ec: v.tensor_scalar_mul(tmpb[:], xpadv[:, 0:TS],
                                                                        convw[:, ec * 4:ec * 4 + 1]),
                         inc=("s_dv", 1))
                for k in range(1, 4):
                    t_c = E_("vector", lambda v, ec=ec, k=k: v.scalar_tensor_tensor(tmpb[:], xpadv[:, k:k + TS],
                                                                                    convw[:, ec * 4 + k:ec * 4 + k + 1],
                                                                                    tmpb[:], op0=ALU.mult, op1=ALU.add),
                             inc=("s_dv", 1))
                t_sl = E_("scalar", lambda sc, sv=sv, ec=ec: sc.activation(sv, tmpb[:], AF.Silu,
                                                                           bias=convb[:, ec:ec + 1]),
                          waits=[t_c, pr.tok("s_sp"), pr.tok("s_x"), pr.tok("s_dv")], inc=("s_ac", 1))
                E_("sync", lambda s, sv=sv, ec=ec: s.dma_start(sp_xs[ec], sv), waits=[t_sl], inc=("s_sp", 16))
                E_("vector", lambda v, sv=sv, ec=ec: v.tensor_copy(xsbv[:, ec * TS:(ec + 1) * TS], sv),
                   waits=[t_sl] + ([prev_wB4_free] if (ec == 0 and prev_wB4_free) else []), inc=("s_dv", 1))
            else:
                ec = m - NEC
                t_sl = E_("scalar", lambda sc, sv=sv, ps=ps: sc.activation(sv, ps[:], AF.Silu),
                          waits=[t_mm, pr.tok("s_sp"), pr.tok("s_x"), pr.tok("s_dv")], inc=("s_ac", 1))
                ps_free[psn] = [t_sl]
                E_("sync", lambda s, sv=sv, ec=ec: s.dma_start(sp_z[ec], sv), waits=[t_sl], inc=("s_sp", 16))
        t_phB_sp = pr.tok("s_sp")
        t_phB_dv = pr.tok("s_dv")

        t_last_inproj_mm = pr.tok("s_pe")
        t_w_dt = None
        for ec in range(NEC):
            t_w_dt = E_("sync", lambda s, ec=ec: s.dma_start(w8v(ec * 1024, 1024),
                                                             sc_dtwT[ec * 128:(ec + 1) * 128, 0:1024]),
                        waits=[t_last_inproj_mm] + ([ct["dtw"]] if first else []), inc=("s_w", 16))

        if stop_i < PHASES.index("Ca"):
            continue
        for m in range(NEC):
            if m == NEC // 2:
                t_half_pe = pr.tok("s_pe")
                for ec in range(NEC):
                    t_w_dt = E_("sync", lambda s, ec=ec:
                                s.dma_start(w8v(ec * 1024, 1024),
                                            sc_dtwT[ec * 128:(ec + 1) * 128, 1024:2048]),
                                waits=[t_half_pe], inc=("s_w", 16))
            psn = "psA" if m % 2 == 0 else "psB"
            ps = PS[psn]
            ml = m % (NEC // 2)
            t_mm = E_("tensor", lambda t, ml=ml, ps=ps: [t.matmul(ps[:, h * 512:(h + 1) * 512],
                                                                  w8v(ec * 1024 + ml * 128, 128),
                                                                  xsbv[:, ec * TS + h * 512: ec * TS + (h + 1) * 512],
                                                                  start=(ec == 0), stop=(ec == NEC - 1))
                                                         for h in range(2) for ec in range(NEC)][-1],
                      waits=[t_w_dt, t_phB_dv] + ps_free[psn], inc=("s_pe", 1))
            if ca_mode >= 2:
                t_a1 = E_("scalar", lambda sc, ps=ps, m=m: sc.activation(tmpa[:], ps[:], AF.Abs, bias=dtb[:, m:m + 1]),
                          waits=[t_mm, pr.tok("s_dv")], inc=("s_ac", 1))
                E_("scalar", lambda sc: sc.activation(tmpa[:], tmpa[:], AF.Exp, scale=-1.0), inc=("s_ac", 1))
                t_a3 = E_("scalar", lambda sc: sc.activation(tmpa[:], tmpa[:], AF.Ln, bias=1.0), inc=("s_ac", 1))
            else:
                t_a1 = E_("scalar", lambda sc, ps=ps, m=m: sc.copy(tmpa[:], ps[:]),
                          waits=[t_mm, pr.tok("s_dv")], inc=("s_ac", 1))
                t_a3 = t_a1
            slot = m % 2
            sv = stg[:, slot * TS:(slot + 1) * TS]
            if ca_mode >= 1:
                t_mx = E_("vector", lambda v, ps=ps, m=m, sv=sv: v.tensor_scalar(sv, ps[:], dtb[:, m:m + 1], 0.0,
                                                                                 op0=ALU.add, op1=ALU.max),
                          waits=[t_mm, t_a1, pr.tok("s_sp")], inc=("s_dv", 1))
            else:
                t_mx = E_("vector", lambda v, sv=sv: v.tensor_copy(sv, tmpb[:]),
                          waits=[t_mm, pr.tok("s_sp"), pr.tok("s_ac")], inc=("s_dv", 1))
            ps_free[psn] = [t_mx, t_a1]
            t_dt = E_("vector", lambda v, sv=sv: v.tensor_add(sv, sv, tmpa[:]), waits=[t_a3], inc=("s_dv", 1))
            E_("sync", lambda s, sv=sv, m=m: s.dma_start(sp_dt[m], sv), waits=[t_dt], inc=("s_sp", 16))
        if stop_i < PHASES.index("C"):
            continue
        t_bc_mm = E_("tensor", lambda t: [t.matmul(psC[0:32, h * 512:(h + 1) * 512],
                                                   bcT[:, ec * 32:(ec + 1) * 32],
                                                   xsbv[:, ec * TS + h * 512: ec * TS + (h + 1) * 512],
                                                   start=(ec == 0), stop=(ec == NEC - 1))
                                          for h in range(2) for ec in range(NEC)][-1],
                     waits=[t_phB_dv, t_p0_ac] + ([t_gp_all] if first else []) + ps_free["psC"], inc=("s_pe", 1))
        t_bc_ev = E_("scalar", lambda sc: sc.copy(stg[0:32, 0:TS], psC[0:32, :]),
                     waits=[t_bc_mm, pr.tok("s_sp")], inc=("s_ac", 1))
        ps_free["psC"] = [t_bc_ev]
        t_bc_st = E_("sync", lambda s: s.dma_start(sc_bc, stg[0:32, 0:TS]), waits=[t_bc_ev], inc=("s_sp", 16))
        t_phC_sp = pr.tok("s_sp")

        if debug and first:
            E_("sync", lambda s: s.dma_start(d_tap_bc, stg[0:32, 0:TS]), waits=[t_bc_ev], inc=("s_tap", 16))

        t_phC_pe = pr.tok("s_pe")
        t_w_outp = None
        for ec in range(NEC):
            t_w_outp = E_("sync", lambda s, ec=ec: s.dma_start(wB4v(ec * 1024, DM),
                                                               sc_outpT[ec * 128:(ec + 1) * 128, :]),
                          waits=[t_phC_pe] + ([ct["outp"]] if first else []), inc=("s_w", 16))

        if stop_i < PHASES.index("D"):
            continue
        if half == 0:
            E_("vector", lambda v: v.memset(hstate[:], 0.0), inc=("s_dv", 1))
        for g in range(NG):
            t_rl_grp = None
            for i in range(ECG):
                ec = g * ECG + i
                E_("sync", lambda s, ec=ec, i=i: s.dma_start(dtg(i), sp_dt[ec]),
                   waits=[t_phC_sp, pr.tok("s_dv"), pr.tok("s_pe")], inc=("s_rl", 16))
                t_rl_grp = E_("sync", lambda s, ec=ec, i=i: s.dma_start(rlbuf[:, i * TS:(i + 1) * TS], sp_xs[ec]),
                              waits=[t_phB_sp, pr.tok("s_dv"), pr.tok("s_ac")], inc=("s_rl", 16))
            rl_toks = []
            for i in range(ECG):
                t_u = E_("vector", lambda v, i=i: v.tensor_mul(ug(i), dtg(i), rlbuf[:, i * TS:(i + 1) * TS]),
                         waits=[t_rl_grp], inc=("s_dv", 1))
                rl_toks.append(t_u)
            bc_tok = [None, None]

            def bcld(n, wdv):
                q = n % 2
                sem = "s_b0" if q == 0 else "s_b1"
                E_("sync", lambda s, n=n, q=q: s.dma_start(bcrep[:, q * 2 * TS:q * 2 * TS + TS],
                                                           sc_bc[n:n + 1, :].broadcast_to([128, TS])),
                   waits=[t_bc_st] + wdv, inc=(sem, 16))
                bc_tok[q] = E_("sync", lambda s, n=n, q=q:
                               s.dma_start(bcrep[:, q * 2 * TS + TS:(q + 1) * 2 * TS],
                                           sc_bc[NST + n:NST + n + 1, :].broadcast_to([128, TS])),
                               inc=(sem, 16))
            t_dv0 = pr.tok("s_dv")
            bcld(0, [t_dv0])
            bcld(1, [t_dv0])
            for n in range(NST):
                q = n % 2
                bB = bcrep[:, q * 2 * TS:q * 2 * TS + TS]
                bC = bcrep[:, q * 2 * TS + TS:(q + 1) * 2 * TS]
                t_bC = bc_tok[q]
                for i in range(ECG):
                    ec = g * ECG + i
                    col = ec * NST + n
                    t_aa = E_("scalar", lambda sc, i=i, col=col: sc.activation(tmpa[:], dtg(i), AF.Exp,
                                                                               scale=A_sb[:, col:col + 1]),
                              waits=[rl_toks[i], pr.tok("s_dv")], inc=("s_ac", 1))
                    E_("vector", lambda v, i=i, bB=bB: v.tensor_mul(tmpb[:], ug(i), bB),
                       waits=[t_bC], inc=("s_dv", 1))
                    t_scn = E_("vector", lambda v, col=col: v.tensor_tensor_scan(tmph[:], tmpa[:], tmpb[:],
                                                                                 hstate[:, col:col + 1],
                                                                                 op0=ALU.mult, op1=ALU.add),
                               waits=[t_aa], inc=("s_dv", 1))
                    E_("vector", lambda v, col=col: v.tensor_copy(hstate[:, col:col + 1], tmph[:, TS - 1:TS]),
                       waits=[t_scn], inc=("s_dv", 1))
                    if n == 0:
                        E_("vector", lambda v, i=i, bC=bC: v.tensor_mul(ysg(i), tmph[:], bC),
                           waits=[t_bC], inc=("s_dv", 1))
                    else:
                        E_("vector", lambda v, bC=bC: v.tensor_mul(tmpa[:], tmph[:], bC),
                           waits=[t_bC], inc=("s_dv", 1))
                        E_("vector", lambda v, i=i: v.tensor_add(ysg(i), ysg(i), tmpa[:]), inc=("s_dv", 1))
                if n + 2 < NST:
                    bcld(n + 2, [pr.tok("s_dv")])
            t_rl2 = None
            for i in range(ECG):
                ec = g * ECG + i
                E_("sync", lambda s, ec=ec, i=i: s.dma_start(rlbuf[:, i * TS:(i + 1) * TS], sp_xs[ec]),
                   waits=[pr.tok("s_dv")], inc=("s_rl", 16))
                t_rl2 = E_("sync", lambda s, ec=ec, i=i: s.dma_start(stg[:, i * TS:(i + 1) * TS], sp_z[ec]),
                           waits=[t_phC_sp, pr.tok("s_ac"), pr.tok("s_dv"), pr.tok("s_tap")], inc=("s_rl", 16))
            for i in range(ECG):
                ec = g * ECG + i
                t_y2 = E_("vector", lambda v, ec=ec, i=i:
                          v.scalar_tensor_tensor(ysg(i), rlbuf[:, i * TS:(i + 1) * TS],
                                                 D_sb[:, ec:ec + 1], ysg(i), op0=ALU.mult, op1=ALU.add),
                          waits=[t_rl2], inc=("s_dv", 1))
                slot = ec % 2
                t_ym = E_("vector", lambda v, i=i, slot=slot:
                          v.tensor_mul(ymst[:, slot * TS:(slot + 1) * TS], ysg(i), stg[:, i * TS:(i + 1) * TS]),
                          waits=[t_rl2, pr.tok("s_ym")], inc=("s_dv", 1))
                E_("sync", lambda s, ec=ec, slot=slot: s.dma_start(sp_ym[ec], ymst[:, slot * TS:(slot + 1) * TS]),
                   waits=[t_ym], inc=("s_ym", 16))
        t_ym_all = pr.tok("s_ym")

        if stop_i < PHASES.index("E"):
            continue
        for tc in range(NTC):
            t_yl = None
            for ec in range(NEC):
                t_yl = E_("sync", lambda s, ec=ec, tc=tc: s.dma_start(ymst[:, ec * 128:(ec + 1) * 128],
                                                                     sp_ym[ec][:, tc * 128:(tc + 1) * 128]),
                          waits=[t_ym_all] + ([pr.tok("s_pe"), pr.tok("s_dv")] if ec == 0 else []),
                          inc=("s_rl", 16))
            psn = "psA" if tc % 2 == 0 else "psB"
            ps = PS[psn]
            t_mm = E_("tensor", lambda t, ps=ps, t_yl=t_yl: [t.matmul(ps[:, h * 512:(h + 1) * 512],
                                                                     ymst[:, ec * 128:(ec + 1) * 128],
                                                                     wB4v(ec * 1024 + h * 512, 512),
                                                                     start=(ec == 0), stop=(ec == NEC - 1))
                                                             for h in range(2) for ec in range(NEC)][-1],
                      waits=[t_yl, t_w_outp] + ps_free[psn], inc=("s_pe", 1))
            if "attn" in parts:
                t_dep = E_("vector", lambda v, tc=tc, ps=ps: v.tensor_copy(yacc[:, tc * DM:(tc + 1) * DM], ps[:]),
                           waits=[t_mm, pr.tok("s_out")], inc=("s_dv", 1))
            else:
                t_dep = E_("vector", lambda v, tc=tc: v.memset(yacc[:, tc * DM:(tc + 1) * DM], 0.0),
                           waits=[t_mm, pr.tok("s_out")], inc=("s_dv", 1))
            ps_free[psn] = [t_dep]
        t_phE_pe = pr.tok("s_pe")

        if debug and first:
            tdv = pr.tok("s_dv")
            E_("sync", lambda s: s.dma_start(d_tap_att, yacc[:]), waits=[tdv], inc=("s_tap", 16))

        if stop_i < PHASES.index("F"):
            continue
        if "moe" in parts:
            for e in range(NE):
                wf_wait = [t_phE_pe, pr.tok("s_dv"), pr.tok("s_tap")] if e == 0 else [pr.tok("s_pe")]
                if first:
                    wf_wait = wf_wait + [ct[("fc1", e)], ct[("fc2", e)]]
                tw1 = None
                for dmc in range(NDMC):
                    tw1 = E_("sync", lambda s, dmc=dmc, e=e: s.dma_start(w8v(dmc * 2048, 2 * FF),
                                                                        sc_fc1T[e][dmc * 128:(dmc + 1) * 128, :]),
                             waits=wf_wait, inc=("s_w", 16))
                tw2 = None
                for fc in range(8):
                    tw2 = E_("sync", lambda s, fc=fc, e=e: s.dma_start(w8v(16384 + fc * 1024, DM),
                                                                      sc_fc2T[e][fc * 128:(fc + 1) * 128, :]),
                             waits=wf_wait, inc=("s_w", 16))
                t_sc2 = None
                for fc in range(8):
                    t_sc2 = E_("vector", lambda v, fc=fc, e=e: v.tensor_scalar_mul(w8v(16384 + fc * 1024, DM),
                                                                                   w8v(16384 + fc * 1024, DM),
                                                                                   gmult[:, e * 8 + fc:e * 8 + fc + 1]),
                               waits=[tw2], inc=("s_dv", 1))
                for j in range(8):
                    psgn, psxn = ("psA", "psB") if j % 2 == 0 else ("psB", "psA")
                    psg_, psx_ = PS[psgn], PS[psxn]
                    t_mmg = E_("tensor", lambda t, j=j, psg_=psg_: [t.matmul(psg_[:, h * 512:(h + 1) * 512],
                                                                             w8v(dmc * 2048 + (2 * j + 1) * 128, 128),
                                                                             xnT[:, dmc * TS + h * 512: dmc * TS + (h + 1) * 512],
                                                                             start=(dmc == 0), stop=(dmc == NDMC - 1))
                                                                    for h in range(2) for dmc in range(NDMC)][-1],
                               waits=[tw2] + ps_free[psgn], inc=("s_pe", 1))
                    t_gl = E_("scalar", lambda sc, psg_=psg_, e=e, j=j:
                              sc.activation(stg[:, 0:TS], psg_[:], AF.Gelu,
                                            bias=fc1bs[:, e * 16 + 8 + j:e * 16 + 9 + j]),
                              waits=[t_mmg, pr.tok("s_dv"), pr.tok("s_rl")], inc=("s_ac", 1))
                    ps_free[psgn] = [t_gl]
                    t_mmx = E_("tensor", lambda t, j=j, psx_=psx_: [t.matmul(psx_[:, h * 512:(h + 1) * 512],
                                                                             w8v(dmc * 2048 + 2 * j * 128, 128),
                                                                             xnT[:, dmc * TS + h * 512: dmc * TS + (h + 1) * 512],
                                                                             start=(dmc == 0), stop=(dmc == NDMC - 1))
                                                                    for h in range(2) for dmc in range(NDMC)][-1],
                               waits=[tw2] + ps_free[psxn], inc=("s_pe", 1))
                    t_ap = E_("vector", lambda v, psx_=psx_, e=e, j=j:
                              v.scalar_tensor_tensor(apbv(j), psx_[:],
                                                     fc1bs[:, e * 16 + j:e * 16 + j + 1], stg[:, 0:TS],
                                                     op0=ALU.add, op1=ALU.mult),
                              waits=[t_mmx, t_gl], inc=("s_dv", 1))
                    ps_free[psxn] = [t_ap]
                t_ap_done = pr.tok("s_dv")
                for tc in range(NTC):
                    t_mm2 = E_("tensor", lambda t, tc=tc: [t.matmul(psC[:, h * 512:(h + 1) * 512],
                                                                    apbv(j)[:, tc * 128:(tc + 1) * 128],
                                                                    w8v(16384 + j * 1024 + h * 512, 512),
                                                                    start=(j == 0), stop=(j == 7))
                                                           for h in range(2) for j in range(8)][-1],
                               waits=[t_sc2, t_ap_done] + ps_free["psC"], inc=("s_pe", 1))
                    t_dep = E_("vector", lambda v, tc=tc, e=e:
                               v.scalar_tensor_tensor(yacc[:, tc * DM:(tc + 1) * DM], psC[:],
                                                      comb[:, tc * NE + e:tc * NE + e + 1],
                                                      yacc[:, tc * DM:(tc + 1) * DM],
                                                      op0=ALU.mult, op1=ALU.add),
                               waits=[t_mm2, t_comb_done], inc=("s_dv", 1))
                    ps_free["psC"] = [t_dep]
        t_phF_pe = pr.tok("s_pe")
        t_phF_dv = pr.tok("s_dv")

        if stop_i < PHASES.index("G"):
            continue
        if "shared" in parts:
            tw_s2 = None
            for fsc in range(NEC):
                tw_s2 = E_("sync", lambda s, fsc=fsc: s.dma_start(wB4v(fsc * 1024, DM),
                                                                  sc_sfc2T[fsc * 128:(fsc + 1) * 128, :]),
                           waits=[t_phF_pe] + ([ct["sfc2"]] if first else []), inc=("s_w2", 16))
            t_ssc = None
            for fsc in range(NEC):
                t_ssc = E_("vector", lambda v, fsc=fsc: v.tensor_scalar_mul(wB4v(fsc * 1024, DM),
                                                                            wB4v(fsc * 1024, DM),
                                                                            smult[:, fsc:fsc + 1]),
                           waits=[tw_s2, t_phF_dv], inc=("s_dv", 1))
            slot_tok = [None, None]
            slot_sem_g = ["s_wa", "s_wb"]
            for j in range(NEC):
                sl = j % 2
                base = 16384 + sl * 2048
                t_wj = None
                for dmc in range(NDMC):
                    w0 = ([t_phF_pe if j < 2 else pr.tok("s_pe")] +
                          ([slot_tok[sl]] if slot_tok[sl] else []) +
                          ([pr.tok("s_dv")] if j == 0 else []) +
                          ([ct["sfc1"]] if (first and j < 2) else [])) if dmc == 0 else []
                    t_wj = E_("sync", lambda s, j=j, base=base, dmc=dmc:
                              s.dma_start(w8[:, base + dmc * 256: base + dmc * 256 + 256],
                                          sc_sfc1T[dmc * 128:(dmc + 1) * 128, :]
                                          .rearrange("p (s a c) -> p a s c", s=2, c=128)[:, j]),
                              waits=w0, inc=(slot_sem_g[sl], 16))
                psgn, psxn = ("psA", "psB") if j % 2 == 0 else ("psB", "psA")
                psg_, psx_ = PS[psgn], PS[psxn]
                t_mmg = E_("tensor", lambda t, base=base, psg_=psg_: [t.matmul(psg_[:, h * 512:(h + 1) * 512],
                                                                               w8[:, base + dmc * 256 + 128: base + dmc * 256 + 256],
                                                                               xnT[:, dmc * TS + h * 512: dmc * TS + (h + 1) * 512],
                                                                               start=(dmc == 0), stop=(dmc == NDMC - 1))
                                                                      for h in range(2) for dmc in range(NDMC)][-1],
                           waits=[t_wj] + ps_free[psgn], inc=("s_pe", 1))
                t_gl = E_("scalar", lambda sc, psg_=psg_, j=j: sc.activation(stg[:, 0:TS], psg_[:], AF.Gelu,
                                                                             bias=sfc1bs[:, 16 + j:17 + j]),
                          waits=[t_mmg, pr.tok("s_dv")], inc=("s_ac", 1))
                ps_free[psgn] = [t_gl]
                t_mmx = E_("tensor", lambda t, base=base, psx_=psx_: [t.matmul(psx_[:, h * 512:(h + 1) * 512],
                                                                               w8[:, base + dmc * 256: base + dmc * 256 + 128],
                                                                               xnT[:, dmc * TS + h * 512: dmc * TS + (h + 1) * 512],
                                                                               start=(dmc == 0), stop=(dmc == NDMC - 1))
                                                                      for h in range(2) for dmc in range(NDMC)][-1],
                           waits=[t_wj] + ps_free[psxn], inc=("s_pe", 1))
                t_as = E_("vector", lambda v, psx_=psx_, j=j:
                          v.scalar_tensor_tensor(w8[:, j * 1024:(j + 1) * 1024],
                                                 psx_[:], sfc1bs[:, j:j + 1], stg[:, 0:TS],
                                                 op0=ALU.add, op1=ALU.mult),
                          waits=[t_mmx, t_gl], inc=("s_dv", 1))
                slot_tok[sl] = t_mmx
                ps_free[psxn] = [t_as]
            t_as_done = pr.tok("s_dv")
            for tc in range(NTC):
                E_("tensor", lambda t, tc=tc: [t.matmul(psC[:, h * 512:(h + 1) * 512],
                                                        w8[:, j * 1024 + tc * 128: j * 1024 + (tc + 1) * 128],
                                                        wB4v(j * 1024 + h * 512, 512),
                                                        start=(j == 0), stop=False)
                                               for h in range(2) for j in range(NEC)][-1],
                   waits=[t_as_done, t_ssc] + ps_free["psC"], inc=("s_pe", 1))
                if "moe" in parts:
                    E_("tensor", lambda t, tc=tc: [t.matmul(psC[:, h * 512:(h + 1) * 512],
                                                            combT[0:NE, tc * 128:(tc + 1) * 128],
                                                            fc2bias[0:NE, h * 512:(h + 1) * 512],
                                                            start=False, stop=False) for h in range(2)][-1],
                       waits=[t_xnT_done, t_gp_all], inc=("s_pe", 1))
                t_mmb2 = E_("tensor", lambda t, tc=tc: [t.matmul(psC[:, h * 512:(h + 1) * 512],
                                                                 onesb[0:1, 0:128],
                                                                 sfc2bias[0:1, h * 512:(h + 1) * 512],
                                                                 start=False, stop=True) for h in range(2)][-1],
                            waits=[t_gp_all], inc=("s_pe", 1))
                t_dep = E_("vector", lambda v, tc=tc: v.tensor_add(yacc[:, tc * DM:(tc + 1) * DM],
                                                                   yacc[:, tc * DM:(tc + 1) * DM], psC[:]),
                           waits=[t_mmb2], inc=("s_dv", 1))
                ps_free["psC"] = [t_dep]

        if stop_i < PHASES.index("H"):
            continue
        for tc in range(NTC):
            slot = tc % 2
            xv = stg[:, slot * TS: slot * TS + DM]
            if "resid" in parts:
                t_x = E_("sync", lambda s, b=b, tb=tb, tc=tc, xv=xv:
                         s.dma_start(xv, d_x[b, tb + tc * 128: tb + (tc + 1) * 128, :]),
                         waits=[pr.tok("s_dv"), pr.tok("s_ac")], inc=("s_x", 16))
                t_add = E_("vector", lambda v, tc=tc, xv=xv: v.tensor_add(yacc[:, tc * DM:(tc + 1) * DM],
                                                                          yacc[:, tc * DM:(tc + 1) * DM], xv),
                           waits=[t_x], inc=("s_dv", 1))
            else:
                t_add = pr.tok("s_dv")
            E_("sync", lambda s, b=b, tb=tb, tc=tc: s.dma_start(d_out[b, tb + tc * 128: tb + (tc + 1) * 128, :],
                                                                yacc[:, tc * DM:(tc + 1) * DM]),
               waits=[t_add], inc=("s_out", 16))

        prev_w8_free = pr.tok("s_pe")
        prev_wB4_free = pr.tok("s_pe")

    E_("vector", lambda v: None,
       waits=[pr.tok(s) for s in ("s_out", "s_in", "s_w", "s_wt", "s_wa", "s_wb",
                                  "s_w2", "s_sp", "s_rl", "s_bc", "s_b0", "s_b1",
                                  "s_gp", "s_ym", "s_x", "s_tap")])

    with stack:
        with nc.Block() as blk:
            OWN = {"vector": "s_dv", "scalar": "s_ac", "gpsimd": "s_gp"}

            def realize(engname):
                own = OWN.get(engname)

                def body(eng):
                    prev = 0
                    for waits, fn, incv in pr.ops[engname]:
                        seen = False
                        for (s, v) in waits:
                            if own is not None and s == own:
                                v = max(v, prev)
                                seen = True
                            eng.wait_ge(sems[s], v)
                        if own is not None and not seen and prev > 0:
                            eng.wait_ge(sems[own], prev)
                        inst = fn(eng) if fn is not None else None
                        if isinstance(inst, list):
                            inst = inst[-1]
                        if incv is not None:
                            sname, delta = incv
                            assert inst is not None, f"inc without instruction on {engname}"
                            inst.then_inc(sems[sname], delta)
                            if sname == own:
                                prev += delta
                return body

            blk.sync(realize("sync"))
            blk.vector(realize("vector"))
            blk.scalar(realize("scalar"))
            blk.tensor(realize("tensor"))
            blk.gpsimd(realize("gpsimd"))
    return nc


def make_in_maps(inputs):
    import ml_dtypes
    f32 = np.float32
    x = np.ascontiguousarray(inputs["L0_attn_x"], dtype=f32)
    shards = np.split(x, 8, axis=0)

    def A(v):
        return np.ascontiguousarray(v, dtype=f32)

    def colmajor(v, k):
        return A(np.asarray(v, dtype=f32).reshape(k, 128).T)

    inproj = np.asarray(inputs["L0_attn_in_proj_w"], dtype=f32)
    dtw = np.asarray(inputs["L0_attn_dt_w"], dtype=f32)
    outp = np.asarray(inputs["L0_attn_out_proj_w"], dtype=f32)
    sfc1 = np.asarray(inputs["L0_ffn_s_fc1_w"], dtype=f32)
    sfc2 = np.asarray(inputs["L0_ffn_s_fc2_w"], dtype=f32)
    fc1 = np.asarray(inputs["L0_ffn_fc1_w"], dtype=f32)
    fc2 = np.asarray(inputs["L0_ffn_fc2_w"], dtype=f32)
    gate = np.asarray(inputs["L0_ffn_gate_w"], dtype=f32)
    wB = np.asarray(inputs["L0_attn_B_proj_w"], dtype=f32)
    wC = np.asarray(inputs["L0_attn_C_proj_w"], dtype=f32)
    Alog = np.asarray(inputs["L0_attn_A_log"], dtype=f32)
    convw = np.asarray(inputs["L0_attn_conv_w"], dtype=f32)
    gmult = np.asarray(inputs["L0_ffn_geglu_mult"], dtype=f32)
    fc1b = np.asarray(inputs["L0_ffn_fc1_b"], dtype=f32)

    fc1T = fc1.transpose(0, 2, 1)
    fc1P = fc1T.reshape(NE, DM, 2, 8, 128).transpose(0, 1, 3, 2, 4).reshape(NE, DM, 2 * FF)

    base = {
        "w_inprojT": A(inproj.T),
        "w_dtT": A(dtw.T),
        "w_outpT": A(outp.T),
        "w_sfc1T": A(sfc1.T),
        "w_sfc2T": A(sfc2.T),
        "w_fc1P": A(fc1P),
        "w_fc2T": A(fc2.transpose(0, 2, 1)),
        "w_gateT": A(gate.T.reshape(NDMC, 128, NE).transpose(1, 0, 2).reshape(128, NDMC * NE)),
        "w_bcT": A(np.concatenate(
            [wB.T.reshape(NEC, 128, NST).transpose(1, 0, 2),
             wC.T.reshape(NEC, 128, NST).transpose(1, 0, 2)], axis=2).reshape(128, NEC * 2 * NST)),
        "A_log2": A(Alog.reshape(NEC, 128, NST).transpose(1, 0, 2).reshape(128, NEC * NST)),
        "w_conv2": A(convw[:, 0, :].reshape(NEC, 128, 4).transpose(1, 0, 2).reshape(128, NEC * 4)),
        "b_conv2": colmajor(inputs["L0_attn_conv_b"], NEC),
        "b_dt2": colmajor(inputs["L0_attn_dt_b"], NEC),
        "D_par2": colmajor(inputs["L0_attn_D_param"], NEC),
        "b_sfc1_2": colmajor(inputs["L0_ffn_s_fc1_b"], 32),
        "s_mult2": colmajor(inputs["L0_ffn_s_geglu_mult"], NST),
        "g_mult2": A(gmult.reshape(NE, 8, 128).transpose(2, 0, 1).reshape(128, NE * 8)),
        "b_fc1_2": A(fc1b.reshape(NE, 16, 128).transpose(2, 0, 1).reshape(128, NE * 16)),
        "b_fc2": A(inputs["L0_ffn_fc2_b"]),
        "b_sfc2": A(inputs["L0_ffn_s_fc2_b"]),
    }
    aux = {
        "ident_bf": np.eye(128, dtype=ml_dtypes.bfloat16),
        "ones_bf": np.ones((1, 128), dtype=ml_dtypes.bfloat16),
    }
    return [dict(base, x=s, **aux) for s in shards]


def kernel(**inputs):
    nc = build_core_program()
    in_maps = make_in_maps(inputs)
    res = run_bass_kernel_spmd(nc, in_maps, core_ids=list(range(8)))
    return np.concatenate([r["out"] for r in res.results], axis=0)
